# version 01 (final)
# baseline (speedup 1.0000x reference)
import contextlib
import math
import numpy as np
import ml_dtypes
import concourse.bass as bass
import concourse.mybir as mybir
from concourse.bass_utils import run_bass_kernel_spmd
from concourse.bass_core_rust import IndirectOffsetOnAxis

F32 = mybir.dt.float32
BF16 = mybir.dt.bfloat16
I32 = mybir.dt.int32
AF = mybir.ActivationFunctionType
ALU = mybir.AluOpType
AX = mybir.AxisListType
DEBUG = False

NCORES = 8
BLOC = 2
T = 2048
D = 1024
NTOK = BLOC * T
NT = NTOK // 128
TPB = T // 128


class Prog:
    def __init__(self):
        self.nc = bass.Bass("TRN2", target_bir_lowering=False)
        self.stack = contextlib.ExitStack()
        self.scopes = []
        self.feeds = {}
        self._names = set()
        self._cache = {}

    def dram(self, name, shape, dtype, kind="Internal"):
        assert name not in self._names
        self._names.add(name)
        return self.nc.dram_tensor(name, list(shape), dtype, kind=kind).ap()

    def dram_in(self, name, arr, dtype=None):
        if dtype is None:
            dtype = {np.dtype(np.float32): F32, np.dtype(np.int32): I32,
                     np.dtype(ml_dtypes.bfloat16): BF16}[arr.dtype]
        ap = self.dram(name, arr.shape, dtype, kind="ExternalInput")
        self.feeds[name] = arr
        return ap

    def dram_in_cached(self, name, arr_fn):
        if name in self._names:
            return self._cache[name]
        ap = self.dram_in(name, arr_fn())
        self._cache[name] = ap
        return ap

    def sem(self, name):
        return self.stack.enter_context(self.nc.semaphore(name))

    def sbuf(self, name, shape, dtype):
        st = self.scopes[-1] if self.scopes else self.stack
        return st.enter_context(self.nc.sbuf_tensor(name, list(shape), dtype))

    def psum(self, name, shape, dtype=F32):
        st = self.scopes[-1] if self.scopes else self.stack
        return st.enter_context(self.nc.psum_tensor(name, list(shape), dtype))

    @contextlib.contextmanager
    def scope(self):
        st = contextlib.ExitStack()
        self.scopes.append(st)
        try:
            yield
        finally:
            self.scopes.pop()
            st.close()

    def close(self):
        self.stack.close()


class Ph:

    def __init__(self, prog, name, prev=None):
        self.prog = prog
        self.nc = prog.nc
        self.name = name
        self.pp = prog.sem(name + "_pp")
        self.n = 0
        self.dq = {}
        self.prev = prev

    def qsem(self, q):
        if q not in self.dq:
            self.dq[q] = [self.prog.sem(f"{self.name}_{q}"), 0]
        return self.dq[q]

    def step(self, block, engine, fn, waits=()):
        n0 = self.n
        prev = self.prev
        def body(eng):
            if prev is not None and n0 == 0:
                eng.wait_ge(prev[0], prev[1])
            if n0 > 0:
                eng.wait_ge(self.pp, n0)
            elif prev is None:
                pass
            for s, v in waits:
                eng.wait_ge(s, v)
            last = fn(eng)
            last.then_inc(self.pp, 1)
        getattr(block, engine)(body)
        self.n += 1

    def dma(self, block, q, fn, waits=(), chain=True):
        qs = self.qsem(q)
        n0 = self.n if chain else None
        prev = self.prev
        eng_name = {"sync": "sync", "gp": "gpsimd", "act": "scalar"}[q]
        def body(eng):
            if prev is not None and self.n == 0:
                pass
            if n0 is not None and n0 > 0:
                eng.wait_ge(self.pp, n0)
            if prev is not None and n0 == 0:
                eng.wait_ge(prev[0], prev[1])
            for s, v in waits:
                eng.wait_ge(s, v)
            def D(out, in_, **kw):
                ins = eng.dma_start(out=out, in_=in_, **kw)
                ins.then_inc(qs[0], 16)
                qs[1] += 16
                return ins
            def DT(out, in_):
                ins = eng.dma_start(out=out, in_=in_, transpose=True)
                ins.then_inc(qs[0], 16)
                qs[1] += 16
                return ins
            def DI(out, out_off, in_, in_off):
                ins = eng.indirect_dma_start(out=out, out_offset=out_off,
                                             in_=in_, in_offset=in_off)
                ins.then_inc(qs[0], 16)
                qs[1] += 16
                return ins
            fn(eng, D, DT, DI)
            if chain:
                eng.nop().then_inc(self.pp, 1)
        getattr(block, eng_name)(body)
        if chain:
            self.n += 1
        return qs

    def qv(self, q):
        qs = self.qsem(q)
        return (qs[0], qs[1])

    def done(self):
        return (self.pp, self.n)


def ln_steps2(ph, block, x_sb, xm_sb, sq_sb, ln_bf, stat, eps_col, waits=()):
    ph.step(block, "vector", lambda v: v.reduce_sum(stat[:, 0:1], x_sb[:], axis=AX.X), waits=waits)
    ph.step(block, "scalar", lambda s: s.activation(stat[:, 1:2], stat[:, 0:1], AF.Copy, scale=1.0 / D))
    ph.step(block, "vector", lambda v: v.tensor_scalar_sub(xm_sb[:], x_sb[:], stat[:, 1:2]))
    ph.step(block, "scalar", lambda s: s.activation(sq_sb[:], xm_sb[:], AF.Square, accum_out=stat[:, 2:3]))
    ph.step(block, "scalar", lambda s: s.activation(stat[:, 2:3], stat[:, 2:3], AF.Sqrt, scale=1.0 / D, bias=eps_col))
    ph.step(block, "vector", lambda v: v.reciprocal(stat[:, 3:4], stat[:, 2:3]))
    ph.step(block, "vector", lambda v: v.tensor_scalar_mul(ln_bf[:], xm_sb[:], stat[:, 3:4]))


def transpose_set(ph, block, src_ap_fn, dst_ap_fn, n, ptr, ident, evac="scalar", waits=()):
    for i0 in range(0, n, 2):
        k = min(2, n - i0)
        def tr(t, i0=i0, k=k):
            for j in range(k):
                ins = t.transpose(ptr[j][:], src_ap_fn(i0 + j), ident)
            return ins
        ph.step(block, "tensor", tr, waits=waits if i0 == 0 else ())
        def cp(e, i0=i0, k=k):
            for j in range(k):
                if evac == "scalar":
                    ins = e.copy(dst_ap_fn(i0 + j), ptr[j][:])
                else:
                    ins = e.tensor_copy(dst_ap_fn(i0 + j), ptr[j][:])
            return ins
        ph.step(block, evac, cp)



def build_consts():
    c = {}
    c["ident_bf"] = np.eye(128, dtype=ml_dtypes.bfloat16)
    c["ident_f32"] = np.eye(128, dtype=np.float32)
    u = np.triu(np.ones((128, 128), np.float32)) / 16.0
    c["gla_U"] = u
    c["gla_mask"] = np.triu(np.ones((128, 128), np.float32)).astype(ml_dtypes.bfloat16)
    s_idx = np.arange(128)
    t_idx = np.arange(256)
    m0 = np.where(s_idx[:, None] <= t_idx[None, :], 0.0, -1e30).astype(np.float32)
    m1 = np.where((s_idx[:, None] + 128) <= t_idx[None, :], 0.0, -1e30).astype(np.float32)
    c["blk_mask"] = np.stack([m0, m1])
    masks = []
    t5 = np.arange(512)
    for r in (0, 128, 256, 384):
        masks.append(np.where((s_idx[:, None] + r) <= t5[None, :], 0.0, -1e30).astype(np.float32))
    c["diag_mask"] = np.stack(masks)
    half = 32
    inv_freq = 1.0 / (10000.0 ** (np.arange(half, dtype=np.float64) / half))
    th = np.arange(T, dtype=np.float64)[:, None] * inv_freq[None, :]
    c["rope_cos"] = np.cos(th).astype(np.float32)
    c["rope_sin"] = np.sin(th).astype(np.float32)
    c["ones_k1"] = np.ones((1, 128), np.float32)
    c["ones_col"] = np.ones((128, 1), np.float32)
    c["iota_col"] = np.arange(128, dtype=np.float32).reshape(128, 1)
    c["iota8"] = np.broadcast_to(np.arange(8, dtype=np.float32), (128, 8)).copy()
    c["Lstrict"] = (s_idx[:, None] < s_idx[None, :]).astype(np.float32)
    c["ones_col_bf"] = np.ones((128, 1), ml_dtypes.bfloat16)
    c["ones_k1_bf"] = np.ones((1, 128), ml_dtypes.bfloat16)
    p = np.arange(128)
    c["tok_i32"] = (p[:, None] + 128 * np.arange(32)[None, :]).astype(np.int32)
    c["tok2a_i32"] = (2 * c["tok_i32"]).astype(np.int32)
    c["tok2b_i32"] = (2 * c["tok_i32"] + 1).astype(np.int32)
    NSLOT = 8 * 1536
    si = np.zeros((NSLOT + 1, 3), np.int32)
    si[:, 0] = 4096
    si[:, 1] = 8192
    c["slab_init"] = si
    c["rope_cos_tm"] = np.ascontiguousarray(
        c["rope_cos"].reshape(16, 128, 32).transpose(1, 0, 2).reshape(128, 512))
    c["rope_sin_tm"] = np.ascontiguousarray(
        c["rope_sin"].reshape(16, 128, 32).transpose(1, 0, 2).reshape(128, 512))
    c["blk_mask_flat"] = np.ascontiguousarray(c["blk_mask"].transpose(1, 0, 2).reshape(128, 512))
    c["diag_mask_flat"] = np.ascontiguousarray(c["diag_mask"].transpose(1, 0, 2).reshape(128, 2048))
    c["diag_mask01"] = np.ascontiguousarray((c["diag_mask"] == 0).astype(ml_dtypes.bfloat16).transpose(1, 0, 2).reshape(128, 2048))
    return c



def phase_ffn1(prog, ph, xin, xout, w):
    nc = prog.nc
    nm = ph.name
    wr_bf = prog.dram(nm + "_wr_bf", [D, D], BF16)
    wk_bf = prog.dram(nm + "_wk_bf", [D, D], BF16)
    wv_bf = prog.dram(nm + "_wv_bf", [D, D], BF16)
    with nc.Block(nm + "_wcast") as block:
        ph.dma(block, "gp", lambda e, Dm, DT, DI: (
            Dm(wr_bf, w["W_r"]), Dm(wk_bf, w["W_k"]), Dm(wv_bf, w["W_v"]))[-1])
    wrt = prog.sbuf(nm + "_wrt", [128, 8 * D], BF16)
    wkt = prog.sbuf(nm + "_wkt", [128, 8 * D], BF16)
    wvt = prog.sbuf(nm + "_wvt", [128, 8 * D], BF16)
    cast_q = ph.qv("gp")
    with nc.Block(nm + "_wload") as block:
        def wl(e, Dm, DT, DI):
            for kk, (sb, dr) in enumerate([(wrt, wr_bf), (wkt, wk_bf), (wvt, wv_bf)]):
                for p in range(8):
                    ins = DT(sb[:, p * D:(p + 1) * D], dr[:, p * 128:(p + 1) * 128])
            return ins
        ph.dma(block, "sync", wl, waits=[cast_q])
    wq = ph.qv("sync")

    ident = prog.sbuf(nm + "_id", [128, 128], BF16)
    identf = prog.sbuf(nm + "_idf", [128, 128], F32)
    mu_r = prog.sbuf(nm + "_mur", [128, 8], F32)
    mu_k = prog.sbuf(nm + "_muk", [128, 8], F32)
    eps_col = prog.sbuf(nm + "_eps", [128, 1], F32)
    with nc.Block(nm + "_cload") as block:
        ph.dma(block, "sync", lambda e, Dm, DT, DI: (
            Dm(ident[:], w["ident_bf"]), Dm(identf[:], w["ident_f32"]),
            Dm(mu_r[:], w["mu_r_fm"]), Dm(mu_k[:], w["mu_k_fm"]))[-1])
        ph.step(block, "vector", lambda v: v.memset(eps_col[:], 1e-5))

    GS = 4
    x_g = prog.sbuf(nm + "_xg", [128, GS * D], F32)
    xm = prog.sbuf(nm + "_xm", [128, D], F32)
    sq = prog.sbuf(nm + "_sq", [128, D], F32)
    stat = prog.sbuf(nm + "_stat", [128, 4], F32)
    ln_bf = prog.sbuf(nm + "_lnbf", [128, GS * D], BF16)
    ln_fm = prog.sbuf(nm + "_lnfm", [128, 8 * 512], BF16)
    lnsh = prog.sbuf(nm + "_lnsh", [128, 8 * 512], BF16)
    carry = prog.sbuf(nm + "_carry", [128, 8], BF16)
    d1 = prog.sbuf(nm + "_d1", [128, 8 * 512], F32)
    d2 = prog.sbuf(nm + "_d2", [128, 8 * 512], F32)
    xr = prog.sbuf(nm + "_xr", [128, 8 * 512], BF16)
    xk = prog.sbuf(nm + "_xk", [128, 8 * 512], BF16)
    r_bf = prog.sbuf(nm + "_rbf", [128, 8 * 512], BF16)
    ka = prog.sbuf(nm + "_ka", [128, 8 * 512], BF16)
    kr = prog.sbuf(nm + "_kr", [128, 512], F32)
    out_fm = prog.sbuf(nm + "_ofm", [128, 8 * 512], F32)
    res = prog.sbuf(nm + "_res", [128, GS * D], F32)

    pA = prog.psum(nm + "_pA", [128, 512], F32)
    pB = prog.psum(nm + "_pB", [128, 512], F32)
    ptr = [prog.psum(nm + "_pt0", [128, 128], BF16), prog.psum(nm + "_pt1", [128, 128], BF16)]
    ptf = [prog.psum(nm + "_pf0", [128, 128], F32), prog.psum(nm + "_pf1", [128, 128], F32)]

    store_mark = None
    NG = NT // GS
    for g in range(NG):
        with nc.Block(f"{nm}_g{g}") as block:
            def ldx(e, Dm, DT, DI, g=g):
                for j in range(GS):
                    ins = Dm(x_g[:, j * D:(j + 1) * D],
                             xin[(g * GS + j) * 128:(g * GS + j + 1) * 128, :])
                return ins
            ph.dma(block, "sync", ldx)
            lw = ph.qv("sync")
            for j in range(GS):
                ln_steps2(ph, block, x_g[:, j * D:(j + 1) * D], xm, sq,
                          ln_bf[:, j * D:(j + 1) * D], stat, eps_col[:],
                          waits=[lw] if j == 0 else ())
            transpose_set(
                ph, block,
                lambda i: ln_bf[:, (i // 8) * D + (i % 8) * 128:(i // 8) * D + (i % 8) * 128 + 128],
                lambda i: ln_fm[:, (i % 8) * 512 + (i // 8) * 128:(i % 8) * 512 + (i // 8) * 128 + 128],
                GS * 8, ptr, ident[:])
            ln3 = ln_fm[:].rearrange("p (c t) -> p c t", c=8)
            sh3 = lnsh[:].rearrange("p (c t) -> p c t", c=8)
            ph.step(block, "vector", lambda v: v.tensor_copy(sh3[:, :, 1:512], ln3[:, :, 0:511]))
            if g % (TPB // GS) == 0:
                ph.step(block, "vector", lambda v: v.memset(sh3[:, :, 0:1], 0.0))
            else:
                ph.step(block, "vector", lambda v: v.tensor_copy(sh3[:, :, 0:1], carry[:].unsqueeze(2)))
            ph.step(block, "vector", lambda v: v.tensor_copy(carry[:].unsqueeze(2), ln3[:, :, 511:512]))
            d13 = d1[:].rearrange("p (c t) -> p c t", c=8)
            d23 = d2[:].rearrange("p (c t) -> p c t", c=8)
            ph.step(block, "vector", lambda v: v.tensor_sub(d13, ln3, sh3))
            ph.step(block, "vector", lambda v: v.tensor_tensor(
                d23, d13, mu_r[:].unsqueeze(2).broadcast_to([128, 8, 512]), op=ALU.mult))
            ph.step(block, "vector", lambda v: v.tensor_add(
                xr[:].rearrange("p (c t) -> p c t", c=8), d23, sh3))
            ph.step(block, "vector", lambda v: v.tensor_tensor(
                d23, d13, mu_k[:].unsqueeze(2).broadcast_to([128, 8, 512]), op=ALU.mult))
            ph.step(block, "vector", lambda v: v.tensor_add(
                xk[:].rearrange("p (c t) -> p c t", c=8), d23, sh3))
            for m in range(8):
                pp = pA if m % 2 == 0 else pB
                def mm(t, m=m, pp=pp):
                    for k in range(8):
                        ins = t.matmul(pp[:], wrt[:, k * D + m * 128:k * D + m * 128 + 128],
                                       xr[:, k * 512:(k + 1) * 512],
                                       start=(k == 0), stop=(k == 7))
                    return ins
                ph.step(block, "tensor", mm, waits=[wq] if (g == 0 and m == 0) else [])
                ph.step(block, "scalar", lambda s, m=m, pp=pp: s.activation(
                    r_bf[:, m * 512:(m + 1) * 512], pp[:], AF.Sigmoid))
            for m in range(8):
                pp = pA if m % 2 == 0 else pB
                def mm(t, m=m, pp=pp):
                    for k in range(8):
                        ins = t.matmul(pp[:], wkt[:, k * D + m * 128:k * D + m * 128 + 128],
                                       xk[:, k * 512:(k + 1) * 512],
                                       start=(k == 0), stop=(k == 7))
                    return ins
                ph.step(block, "tensor", mm)
                ph.step(block, "vector", lambda v, pp=pp: v.tensor_scalar_max(kr[:], pp[:], 0.0))
                ph.step(block, "scalar", lambda s, m=m: s.activation(
                    ka[:, m * 512:(m + 1) * 512], kr[:], AF.Square))
            for m in range(8):
                pp = pA if m % 2 == 0 else pB
                def mm(t, m=m, pp=pp):
                    for k in range(8):
                        ins = t.matmul(pp[:], wvt[:, k * D + m * 128:k * D + m * 128 + 128],
                                       ka[:, k * 512:(k + 1) * 512],
                                       start=(k == 0), stop=(k == 7))
                    return ins
                ph.step(block, "tensor", mm)
                ph.step(block, "vector", lambda v, m=m, pp=pp: v.tensor_mul(
                    out_fm[:, m * 512:(m + 1) * 512], r_bf[:, m * 512:(m + 1) * 512], pp[:]))
            for i0 in range(0, 32, 2):
                def tr(t, i0=i0):
                    for j in range(2):
                        i = i0 + j
                        c, tt = i % 8, i // 8
                        ins = t.transpose(ptf[j][:], out_fm[:, c * 512 + tt * 128:c * 512 + tt * 128 + 128], identf[:])
                    return ins
                ph.step(block, "tensor", tr)
                def add(v, i0=i0):
                    for j in range(2):
                        i = i0 + j
                        c, tt = i % 8, i // 8
                        ins = v.tensor_add(res[:, tt * D + c * 128:tt * D + c * 128 + 128],
                                           x_g[:, tt * D + c * 128:tt * D + c * 128 + 128], ptf[j][:])
                    return ins
                ph.step(block, "vector", add,
                        waits=[store_mark] if (i0 == 0 and store_mark is not None) else [])
            def stx(e, Dm, DT, DI, g=g):
                for j in range(GS):
                    ins = Dm(xout[(g * GS + j) * 128:(g * GS + j + 1) * 128, :],
                             res[:, j * D:(j + 1) * D])
                return ins
            ph.dma(block, "sync", stx)
            store_mark = ph.qv("sync")
    with nc.Block(nm + "_fin") as block:
        ph.step(block, "sync", lambda e: e.nop(), waits=[ph.qv("sync")])
    return ph.done()




def phase_gla1(prog, ph, xin, o_hbm, w):
    nc = prog.nc
    nm = ph.name
    HEADS, HDK, HDV, DK, DV = 4, 256, 512, 1024, 2048
    wq_bf = prog.dram(nm + "_wq", [D, DK], BF16)
    wk_bf = prog.dram(nm + "_wk", [D, DK], BF16)
    wv_bf = prog.dram(nm + "_wv", [D, DV], BF16)
    wg1_bf = prog.dram(nm + "_wg1", [D, 16], BF16)
    wg2_bf = prog.dram(nm + "_wg2", [16, DK], BF16)
    with nc.Block(nm + "_wc") as block:
        ph.dma(block, "gp", lambda e, Dm, DT, DI: (
            Dm(wq_bf, w["Wq"]), Dm(wk_bf, w["Wk"]), Dm(wv_bf, w["Wv"]),
            Dm(wg1_bf, w["Wgk1"]), Dm(wg2_bf, w["Wgk2"]))[-1])
    cast_q = ph.qv("gp")
    wq_sb = prog.sbuf(nm + "_wqs", [128, 8 * DK], BF16)
    wk_sb = prog.sbuf(nm + "_wks", [128, 8 * DK], BF16)
    wv_sb = prog.sbuf(nm + "_wvs", [128, 8 * DV], BF16)
    wg1_sb = prog.sbuf(nm + "_wg1s", [128, 8 * 16], BF16)
    wg2_sb = prog.sbuf(nm + "_wg2s", [16, DK], BF16)
    with nc.Block(nm + "_wl") as block:
        def wl(e, Dm, DT, DI):
            for p in range(8):
                Dm(wq_sb[:, p * DK:(p + 1) * DK], wq_bf[p * 128:(p + 1) * 128, :])
                Dm(wk_sb[:, p * DK:(p + 1) * DK], wk_bf[p * 128:(p + 1) * 128, :])
                Dm(wv_sb[:, p * DV:(p + 1) * DV], wv_bf[p * 128:(p + 1) * 128, :])
                ins = Dm(wg1_sb[:, p * 16:(p + 1) * 16], wg1_bf[p * 128:(p + 1) * 128, :])
            ins = Dm(wg2_sb[:], wg2_bf)
            return ins
        ph.dma(block, "sync", wl, waits=[cast_q])
    wq_ldq = ph.qv("sync")

    ident = prog.sbuf(nm + "_id", [128, 128], BF16)
    U_sb = prog.sbuf(nm + "_U", [128, 128], F32)
    mask_sb = prog.sbuf(nm + "_msk", [128, 128], BF16)
    rms_rep = prog.sbuf(nm + "_rms", [128, DV], F32)
    bg_rep = prog.sbuf(nm + "_bg", [128, DK], F32)
    ccol = prog.sbuf(nm + "_cc", [128, 2], F32)
    with nc.Block(nm + "_cl") as block:
        ph.dma(block, "sync", lambda e, Dm, DT, DI: (
            Dm(ident[:], w["ident_bf"]), Dm(U_sb[:], w["gla_U"]),
            Dm(mask_sb[:], w["gla_mask"]), Dm(rms_rep[:], w["rms_rep"]),
            Dm(bg_rep[:], w["bgk2_rep"]))[-1])
        ph.step(block, "vector", lambda v: v.memset(ccol[:, 0:1], -2.772588722239781))
        ph.step(block, "vector", lambda v: v.memset(ccol[:, 1:2], 1e-5))

    x_sb = prog.sbuf(nm + "_x", [128, D], F32)
    xm = prog.sbuf(nm + "_xm", [128, D], F32)
    sq = prog.sbuf(nm + "_sq", [128, D], F32)
    stat = prog.sbuf(nm + "_st", [128, 4], F32)
    ln_bf = prog.sbuf(nm + "_ln", [128, D], BF16)
    lnT = prog.sbuf(nm + "_lnT", [128, 8 * 128], BF16)
    q_bf = prog.sbuf(nm + "_q", [128, 8 * 128], BF16)
    k_bf = prog.sbuf(nm + "_k", [128, 8 * 128], BF16)
    v_bf = prog.sbuf(nm + "_v", [128, DV], BF16)
    g_tm = prog.sbuf(nm + "_g", [128, DK], F32)
    gkf = prog.sbuf(nm + "_gkf", [128, DK], F32)
    low_bf = prog.sbuf(nm + "_low", [16, 128], BF16)
    eq_sb = prog.sbuf(nm + "_eq", [128, 256], F32)
    ek_sb = prog.sbuf(nm + "_ek", [128, 256], F32)
    ekg_sb = prog.sbuf(nm + "_ekg", [128, 256], F32)
    blast = prog.sbuf(nm + "_bl", [128, 2], F32)
    eS_sb = prog.sbuf(nm + "_eS", [128, 2], F32)
    qg_sb = prog.sbuf(nm + "_qg", [128, 256], BF16)
    kd_sb = prog.sbuf(nm + "_kd", [128, 256], BF16)
    kg_sb = prog.sbuf(nm + "_kg", [128, 256], BF16)
    kgt_sb = prog.sbuf(nm + "_kgt", [128, 256], BF16)
    am_sb = prog.sbuf(nm + "_am", [128, 128], BF16)
    S_f32 = [prog.sbuf(f"{nm}_Sf{h}", [128, 2 * HDV], F32) for h in range(HEADS)]
    S_bf = [prog.sbuf(f"{nm}_Sb{h}", [128, 2 * HDV], BF16) for h in range(HEADS)]
    o_sb = prog.sbuf(nm + "_o", [128, DV], F32)
    o2f = prog.sbuf(nm + "_o2", [128, DV], F32)
    o_out = prog.sbuf(nm + "_oo", [128, DV], BF16)
    osq = prog.sbuf(nm + "_osq", [128, HDV], F32)
    rst = prog.sbuf(nm + "_rst", [128, 8], F32)

    pA = prog.psum(nm + "_pA", [128, 128], F32)
    pB = prog.psum(nm + "_pB", [128, 128], F32)
    p5a = prog.psum(nm + "_p5a", [128, 512], F32)
    p5b = prog.psum(nm + "_p5b", [128, 512], F32)
    pbb0 = prog.psum(nm + "_pb0", [128, 128], F32)
    pbb1 = prog.psum(nm + "_pb1", [128, 128], F32)
    ptr = [prog.psum(nm + "_pt0", [128, 128], BF16), prog.psum(nm + "_pt1", [128, 128], BF16)]

    store_mark = None
    for b in range(BLOC):
        for tt in range(TPB):
            i = b * TPB + tt
            with nc.Block(f"{nm}_t{i}") as block:
                ph.dma(block, "sync", lambda e, Dm, DT, DI, i=i: Dm(
                    x_sb[:], xin[i * 128:(i + 1) * 128, :]))
                lw = ph.qv("sync")
                ln_steps2(ph, block, x_sb[:], xm, sq, ln_bf[:], stat, ccol[:, 1:2], waits=[lw])
                transpose_set(ph, block, lambda j: ln_bf[:, j * 128:(j + 1) * 128],
                              lambda j: lnT[:, j * 128:(j + 1) * 128], 8, ptr, ident[:])
                for (wsb, dst) in ((wq_sb, q_bf), (wk_sb, k_bf)):
                    for mt in range(8):
                        pp = pA if mt % 2 == 0 else pB
                        def mmq(t, wsb=wsb, mt=mt, pp=pp):
                            for k in range(8):
                                ins = t.matmul(pp[:], wsb[:, k * DK + mt * 128:k * DK + mt * 128 + 128],
                                               lnT[:, k * 128:(k + 1) * 128],
                                               start=(k == 0), stop=(k == 7))
                            return ins
                        ph.step(block, "tensor", mmq,
                                waits=[wq_ldq] if (i == 0 and mt == 0 and wsb is wq_sb) else [])
                        ph.step(block, "scalar", lambda s, dst=dst, mt=mt, pp=pp: s.copy(
                            dst[:, mt * 128:(mt + 1) * 128], pp[:]))
                for nt4 in range(4):
                    pp = p5a if nt4 % 2 == 0 else p5b
                    def mmv(t, nt4=nt4, pp=pp):
                        for k in range(8):
                            ins = t.matmul(pp[:], lnT[:, k * 128:(k + 1) * 128],
                                           wv_sb[:, k * DV + nt4 * 512:k * DV + nt4 * 512 + 512],
                                           start=(k == 0), stop=(k == 7))
                        return ins
                    ph.step(block, "tensor", mmv)
                    ph.step(block, "scalar", lambda s, nt4=nt4, pp=pp: s.copy(
                        v_bf[:, nt4 * 512:(nt4 + 1) * 512], pp[:]))
                def mml(t):
                    for k in range(8):
                        ins = t.matmul(pA[0:16, 0:128], wg1_sb[:, k * 16:(k + 1) * 16],
                                       lnT[:, k * 128:(k + 1) * 128],
                                       start=(k == 0), stop=(k == 7))
                    return ins
                ph.step(block, "tensor", mml)
                ph.step(block, "scalar", lambda s: s.copy(low_bf[:], pA[0:16, 0:128]))
                for jb in range(2):
                    pp = p5a if jb == 0 else p5b
                    ph.step(block, "tensor", lambda t, jb=jb, pp=pp: t.matmul(
                        pp[:], low_bf[:], wg2_sb[:, jb * 512:(jb + 1) * 512],
                        start=True, stop=True))
                    ph.step(block, "vector", lambda v, jb=jb, pp=pp: v.tensor_add(
                        gkf[:, jb * 512:(jb + 1) * 512], pp[:], bg_rep[:, jb * 512:(jb + 1) * 512]))
                ph.step(block, "scalar", lambda s: s.activation(g_tm[:], gkf[:], AF.Sigmoid))
                ph.step(block, "scalar", lambda s: s.activation(gkf[:], g_tm[:], AF.Ln))
                for h in range(HEADS):
                    def mmbb(t, h=h):
                        t.matmul(pbb0[:], gkf[:, h * 256:h * 256 + 128], U_sb[:], start=True, stop=True)
                        return t.matmul(pbb1[:], gkf[:, h * 256 + 128:h * 256 + 256], U_sb[:], start=True, stop=True)
                    ph.step(block, "tensor", mmbb)
                    def cpb(v):
                        v.tensor_copy(blast[:, 0:1], pbb0[:, 127:128])
                        return v.tensor_copy(blast[:, 1:2], pbb1[:, 127:128])
                    ph.step(block, "vector", cpb)
                    def acts(s):
                        s.activation(eq_sb[:, 0:128], pbb0[:], AF.Exp, bias=ccol[:, 0:1])
                        s.activation(eq_sb[:, 128:256], pbb1[:], AF.Exp, bias=ccol[:, 0:1])
                        s.activation(ek_sb[:, 0:128], pbb0[:], AF.Exp, scale=-1.0)
                        s.activation(ek_sb[:, 128:256], pbb1[:], AF.Exp, scale=-1.0)
                        s.activation(ekg_sb[:, 0:128], pbb0[:], AF.Exp, scale=-1.0, bias=blast[:, 0:1])
                        s.activation(ekg_sb[:, 128:256], pbb1[:], AF.Exp, scale=-1.0, bias=blast[:, 1:2])
                        return s.activation(eS_sb[:], blast[:], AF.Exp)
                    ph.step(block, "scalar", acts)
                    def muls(v, h=h):
                        v.tensor_mul(qg_sb[:], q_bf[:, h * 256:(h + 1) * 256], eq_sb[:])
                        v.tensor_mul(kd_sb[:], k_bf[:, h * 256:(h + 1) * 256], ek_sb[:])
                        return v.tensor_mul(kg_sb[:], k_bf[:, h * 256:(h + 1) * 256], ekg_sb[:])
                    ph.step(block, "vector", muls)
                    def mmA(t):
                        t.matmul(pA[:], kd_sb[:, 0:128], qg_sb[:, 0:128], start=True, stop=False)
                        return t.matmul(pA[:], kd_sb[:, 128:256], qg_sb[:, 128:256], start=False, stop=True)
                    ph.step(block, "tensor", mmA)
                    ph.step(block, "vector", lambda v: v.tensor_mul(am_sb[:], pA[:], mask_sb[:]))
                    def mmo(t, h=h, first=(tt == 0)):
                        ins = t.matmul(p5a[:], am_sb[:], v_bf[:, h * 512:(h + 1) * 512],
                                       start=True, stop=first)
                        if not first:
                            t.matmul(p5a[:], qg_sb[:, 0:128], S_bf[h][:, 0:512], start=False, stop=False)
                            ins = t.matmul(p5a[:], qg_sb[:, 128:256], S_bf[h][:, 512:1024], start=False, stop=True)
                        return ins
                    ph.step(block, "tensor", mmo)
                    ph.step(block, "vector", lambda v, h=h: v.tensor_copy(
                        o_sb[:, h * 512:(h + 1) * 512], p5a[:]))
                    def trkg(t):
                        t.transpose(ptr[0][:], kg_sb[:, 0:128], ident[:])
                        return t.transpose(ptr[1][:], kg_sb[:, 128:256], ident[:])
                    ph.step(block, "tensor", trkg)
                    def cpkg(s):
                        s.copy(kgt_sb[:, 0:128], ptr[0][:])
                        return s.copy(kgt_sb[:, 128:256], ptr[1][:])
                    ph.step(block, "scalar", cpkg)
                    def mmS(t, h=h):
                        t.matmul(p5a[:], kgt_sb[:, 0:128], v_bf[:, h * 512:(h + 1) * 512], start=True, stop=True)
                        return t.matmul(p5b[:], kgt_sb[:, 128:256], v_bf[:, h * 512:(h + 1) * 512], start=True, stop=True)
                    ph.step(block, "tensor", mmS)
                    if tt == 0:
                        def upd(v, h=h):
                            v.tensor_copy(S_f32[h][:, 0:512], p5a[:])
                            return v.tensor_copy(S_f32[h][:, 512:1024], p5b[:])
                    else:
                        def upd(v, h=h):
                            v.scalar_tensor_tensor(S_f32[h][:, 0:512], S_f32[h][:, 0:512],
                                                   eS_sb[:, 0:1], p5a[:], op0=ALU.mult, op1=ALU.add)
                            return v.scalar_tensor_tensor(S_f32[h][:, 512:1024], S_f32[h][:, 512:1024],
                                                          eS_sb[:, 1:2], p5b[:], op0=ALU.mult, op1=ALU.add)
                    ph.step(block, "vector", upd)
                    def sbf(s, h=h):
                        return s.copy(S_bf[h][:], S_f32[h][:])
                    ph.step(block, "scalar", sbf)
                for h in range(HEADS):
                    ph.step(block, "scalar", lambda s, h=h: s.activation(
                        osq[:], o_sb[:, h * 512:(h + 1) * 512], AF.Square, accum_out=rst[:, h:h + 1]))
                ph.step(block, "scalar", lambda s: s.activation(
                    rst[:, 4:8], rst[:, 0:4], AF.Sqrt, scale=1.0 / HDV, bias=ccol[:, 1:2]))
                ph.step(block, "vector", lambda v: v.reciprocal(rst[:, 0:4], rst[:, 4:8]))
                for h in range(HEADS):
                    ph.step(block, "vector", lambda v, h=h: v.tensor_scalar_mul(
                        o2f[:, h * 512:(h + 1) * 512], o_sb[:, h * 512:(h + 1) * 512], rst[:, h:h + 1]))
                ph.step(block, "vector", lambda v: v.tensor_mul(o_out[:], o2f[:], rms_rep[:]),
                        waits=[store_mark] if store_mark is not None else [])
                ph.dma(block, "sync", lambda e, Dm, DT, DI, i=i: Dm(
                    o_hbm[i * 128:(i + 1) * 128, :], o_out[:]))
                store_mark = ph.qv("sync")
    with nc.Block(nm + "_fin") as block:
        ph.step(block, "sync", lambda e: e.nop(), waits=[ph.qv("sync")])
    return ph.done()


def phase_gla2(prog, ph, xin, o_hbm, xout, w):
    nc = prog.nc
    nm = ph.name
    DV = 2048
    wg_bf = prog.dram(nm + "_wg", [D, DV], BF16)
    wo_bf = prog.dram(nm + "_wo", [DV, D], BF16)
    with nc.Block(nm + "_wc") as block:
        ph.dma(block, "gp", lambda e, Dm, DT, DI: (
            Dm(wg_bf, w["Wg"]), Dm(wo_bf, w["Wo"]))[-1])
    cast_q = ph.qv("gp")
    wg_sb = prog.sbuf(nm + "_wgs", [128, 8 * DV], BF16)
    wo_sb = prog.sbuf(nm + "_wos", [128, 16 * D], BF16)
    with nc.Block(nm + "_wl") as block:
        def wl(e, Dm, DT, DI):
            for p in range(8):
                ins = Dm(wg_sb[:, p * DV:(p + 1) * DV], wg_bf[p * 128:(p + 1) * 128, :])
            for p in range(16):
                ins = Dm(wo_sb[:, p * D:(p + 1) * D], wo_bf[p * 128:(p + 1) * 128, :])
            return ins
        ph.dma(block, "sync", wl, waits=[cast_q])
    w_ldq = ph.qv("sync")
    ident = prog.sbuf(nm + "_id", [128, 128], BF16)
    eps_col = prog.sbuf(nm + "_eps", [128, 1], F32)
    with nc.Block(nm + "_cl") as block:
        ph.dma(block, "sync", lambda e, Dm, DT, DI: Dm(ident[:], w["ident_bf"]))
        ph.step(block, "vector", lambda v: v.memset(eps_col[:], 1e-5))

    x_sb = prog.sbuf(nm + "_x", [128, D], F32)
    xm = prog.sbuf(nm + "_xm", [128, D], F32)
    sq = prog.sbuf(nm + "_sq", [128, D], F32)
    stat = prog.sbuf(nm + "_st", [128, 4], F32)
    ln_bf = prog.sbuf(nm + "_ln", [128, D], BF16)
    lnT = prog.sbuf(nm + "_lnT", [128, 8 * 128], BF16)
    o_tile = prog.sbuf(nm + "_ot", [128, DV], BF16)
    gate_bf = prog.sbuf(nm + "_gt", [128, DV], BF16)
    og_bf = prog.sbuf(nm + "_og", [128, DV], BF16)
    ogT = prog.sbuf(nm + "_ogT", [128, 16 * 128], BF16)
    res = prog.sbuf(nm + "_res", [128, D], F32)
    p5a = prog.psum(nm + "_p5a", [128, 512], F32)
    p5b = prog.psum(nm + "_p5b", [128, 512], F32)
    ptr = [prog.psum(nm + "_pt0", [128, 128], BF16), prog.psum(nm + "_pt1", [128, 128], BF16)]

    store_mark = None
    for i in range(NT):
        with nc.Block(f"{nm}_t{i}") as block:
            def ldx(e, Dm, DT, DI, i=i):
                Dm(x_sb[:], xin[i * 128:(i + 1) * 128, :])
                return Dm(o_tile[:], o_hbm[i * 128:(i + 1) * 128, :])
            ph.dma(block, "sync", ldx)
            lw = ph.qv("sync")
            ln_steps2(ph, block, x_sb[:], xm, sq, ln_bf[:], stat, eps_col[:], waits=[lw])
            transpose_set(ph, block, lambda j: ln_bf[:, j * 128:(j + 1) * 128],
                          lambda j: lnT[:, j * 128:(j + 1) * 128], 8, ptr, ident[:])
            for nt4 in range(4):
                pp = p5a if nt4 % 2 == 0 else p5b
                def mmg(t, nt4=nt4, pp=pp):
                    for k in range(8):
                        ins = t.matmul(pp[:], lnT[:, k * 128:(k + 1) * 128],
                                       wg_sb[:, k * DV + nt4 * 512:k * DV + nt4 * 512 + 512],
                                       start=(k == 0), stop=(k == 7))
                    return ins
                ph.step(block, "tensor", mmg, waits=[w_ldq] if (i == 0 and nt4 == 0) else [])
                ph.step(block, "scalar", lambda s, nt4=nt4, pp=pp: s.activation(
                    gate_bf[:, nt4 * 512:(nt4 + 1) * 512], pp[:], AF.Silu))
            ph.step(block, "vector", lambda v: v.tensor_mul(og_bf[:], o_tile[:], gate_bf[:]))
            transpose_set(ph, block, lambda j: og_bf[:, j * 128:(j + 1) * 128],
                          lambda j: ogT[:, j * 128:(j + 1) * 128], 16, ptr, ident[:])
            for jb in range(2):
                pp = p5a if jb == 0 else p5b
                def mmo(t, jb=jb, pp=pp):
                    for k in range(16):
                        ins = t.matmul(pp[:], ogT[:, k * 128:(k + 1) * 128],
                                       wo_sb[:, k * D + jb * 512:k * D + jb * 512 + 512],
                                       start=(k == 0), stop=(k == 15))
                    return ins
                ph.step(block, "tensor", mmo)
                ph.step(block, "vector", lambda v, jb=jb, pp=pp: v.tensor_add(
                    res[:, jb * 512:(jb + 1) * 512], x_sb[:, jb * 512:(jb + 1) * 512], pp[:]),
                    waits=[store_mark] if (store_mark is not None and jb == 0) else [])
            ph.dma(block, "sync", lambda e, Dm, DT, DI, i=i: Dm(
                xout[i * 128:(i + 1) * 128, :], res[:]))
            store_mark = ph.qv("sync")
    with nc.Block(nm + "_fin") as block:
        ph.step(block, "sync", lambda e: e.nop(), waits=[ph.qv("sync")])
    return ph.done()



def phase_ab0(prog, ph, xin, xout, w):
    nc = prog.nc
    nm = ph.name
    wqkvT_bf = prog.dram(nm + "_wqkvT", [D, 3072], BF16)
    woutT_bf = prog.dram(nm + "_woutT", [D, D], BF16)
    with nc.Block(nm + "_wc") as block:
        ph.dma(block, "gp", lambda e, Dm, DT, DI: (
            Dm(wqkvT_bf, w["WqkvT"]), Dm(woutT_bf, w["WoutT"]))[-1])
    cast_q = ph.qv("gp")
    wqkvT = prog.sbuf(nm + "_wq", [128, 8 * 3072], BF16)
    woutT = prog.sbuf(nm + "_wo", [128, 8 * D], BF16)
    with nc.Block(nm + "_wl") as block:
        def wl(e, Dm, DT, DI):
            for p in range(8):
                Dm(wqkvT[:, p * 3072:(p + 1) * 3072], wqkvT_bf[p * 128:(p + 1) * 128, :])
                ins = Dm(woutT[:, p * D:(p + 1) * D], woutT_bf[p * 128:(p + 1) * 128, :])
            return ins
        ph.dma(block, "sync", wl, waits=[cast_q])
    wq_ldq = ph.qv("sync")
    ident = prog.sbuf(nm + "_id", [128, 128], BF16)
    cos_sb = prog.sbuf(nm + "_cos", [128, 512], F32)
    sin_sb = prog.sbuf(nm + "_sin", [128, 512], F32)
    msk_sb = prog.sbuf(nm + "_msk", [128, 512], F32)
    onc_bf = prog.sbuf(nm + "_onc", [128, 1], BF16)
    onk_bf = prog.sbuf(nm + "_onk", [1, 128], BF16)
    eps_col = prog.sbuf(nm + "_eps", [128, 1], F32)
    with nc.Block(nm + "_cl") as block:
        ph.dma(block, "sync", lambda e, Dm, DT, DI: (
            Dm(ident[:], w["ident_bf"]), Dm(cos_sb[:], w["cos_tm"]),
            Dm(sin_sb[:], w["sin_tm"]), Dm(msk_sb[:], w["blk_mask"]),
            Dm(onc_bf[:], w["ones_col_bf"]), Dm(onk_bf[:], w["ones_k1_bf"]))[-1])
        ph.step(block, "vector", lambda v: v.memset(eps_col[:], 1e-5))

    x_g = prog.sbuf(nm + "_x", [128, 2 * D], F32)
    xm = prog.sbuf(nm + "_xm", [128, D], F32)
    sq = prog.sbuf(nm + "_sq", [128, D], F32)
    stat = prog.sbuf(nm + "_st", [128, 4], F32)
    ln_bf = prog.sbuf(nm + "_ln", [128, D], BF16)
    lnT = prog.sbuf(nm + "_lnT", [128, 8 * 128], BF16)
    qkf = prog.sbuf(nm + "_qkf", [128, 2048], F32)
    v_bf = prog.sbuf(nm + "_v", [128, 2 * D], BF16)
    rp1 = prog.sbuf(nm + "_rp1", [128, 512], F32)
    rp2 = prog.sbuf(nm + "_rp2", [128, 512], F32)
    roq = prog.sbuf(nm + "_roq", [128, D], BF16)
    rok = prog.sbuf(nm + "_rok", [128, D], BF16)
    q_fm = prog.sbuf(nm + "_qfm", [128, 8 * 256], BF16)
    k_fm = prog.sbuf(nm + "_kfm", [128, 8 * 256], BF16)
    probs = prog.sbuf(nm + "_pr", [128, 2 * 256], BF16)
    scf = prog.sbuf(nm + "_scf", [128, 256], F32)
    cs_f = prog.sbuf(nm + "_csf", [1, 256], F32)
    cs_b = prog.sbuf(nm + "_csb", [1, 256], BF16)
    o_fm = prog.sbuf(nm + "_ofm", [128, 8 * 256], BF16)
    o_fm2 = prog.sbuf(nm + "_ofm2", [128, 8 * 256], BF16)
    res = prog.sbuf(nm + "_res", [128, D], F32)

    psc0 = prog.psum(nm + "_ps0", [128, 256], F32)
    psc1 = prog.psum(nm + "_ps1", [128, 256], F32)
    pcs = prog.psum(nm + "_pcs", [1, 256], F32)
    po = prog.psum(nm + "_po", [64, 256], F32)
    p5a = prog.psum(nm + "_p5a", [128, 512], F32)
    p5b = prog.psum(nm + "_p5b", [128, 512], F32)
    ptr = [prog.psum(nm + "_pt0", [128, 128], BF16), prog.psum(nm + "_pt1", [128, 128], BF16)]

    store_mark = None
    NBLK = NTOK // 256
    for blk in range(NBLK):
        with nc.Block(f"{nm}_b{blk}") as block:
            for tt2 in range(2):
                i = blk * 2 + tt2
                tpos = i % TPB
                ph.dma(block, "sync", lambda e, Dm, DT, DI, i=i, tt2=tt2: Dm(
                    x_g[:, tt2 * D:(tt2 + 1) * D], xin[i * 128:(i + 1) * 128, :]))
                lw = ph.qv("sync")
                ln_steps2(ph, block, x_g[:, tt2 * D:(tt2 + 1) * D], xm, sq, ln_bf[:], stat,
                          eps_col[:], waits=[lw])
                transpose_set(ph, block, lambda j: ln_bf[:, j * 128:(j + 1) * 128],
                              lambda j: lnT[:, j * 128:(j + 1) * 128], 8, ptr, ident[:])
                for nt6 in range(6):
                    pp = p5a if nt6 % 2 == 0 else p5b
                    def mmp(t, nt6=nt6, pp=pp):
                        for k in range(8):
                            ins = t.matmul(pp[:], lnT[:, k * 128:(k + 1) * 128],
                                           wqkvT[:, k * 3072 + nt6 * 512:k * 3072 + nt6 * 512 + 512],
                                           start=(k == 0), stop=(k == 7))
                        return ins
                    ph.step(block, "tensor", mmp,
                            waits=[wq_ldq] if (blk == 0 and tt2 == 0 and nt6 == 0) else [])
                    if nt6 < 4:
                        ph.step(block, "vector", lambda v, nt6=nt6, pp=pp: v.tensor_copy(
                            qkf[:, nt6 * 512:(nt6 + 1) * 512], pp[:]))
                    else:
                        ph.step(block, "vector", lambda v, nt6=nt6, pp=pp, tt2=tt2: v.tensor_copy(
                            v_bf[:, tt2 * D + (nt6 - 4) * 512:tt2 * D + (nt6 - 3) * 512], pp[:]))
                cosb = cos_sb[:, tpos * 32:(tpos + 1) * 32].unsqueeze(1).broadcast_to([128, 16, 32])
                sinb = sin_sb[:, tpos * 32:(tpos + 1) * 32].unsqueeze(1).broadcast_to([128, 16, 32])
                for (src0, dst) in ((0, roq), (1, rok)):
                    s3 = qkf[:, src0 * 1024:(src0 + 1) * 1024].rearrange("p (h d) -> p h d", h=16)
                    d3 = dst[:].rearrange("p (h d) -> p h d", h=16)
                    r13 = rp1[:].rearrange("p (h d) -> p h d", h=16)
                    r23 = rp2[:].rearrange("p (h d) -> p h d", h=16)
                    ph.step(block, "vector", lambda v, s3=s3, r13=r13, cosb=cosb: v.tensor_tensor(
                        r13, s3[:, :, 0:64:2], cosb, op=ALU.mult))
                    ph.step(block, "vector", lambda v, s3=s3, r23=r23, sinb=sinb: v.tensor_tensor(
                        r23, s3[:, :, 1:64:2], sinb, op=ALU.mult))
                    ph.step(block, "vector", lambda v, d3=d3, r13=r13, r23=r23: v.tensor_sub(
                        d3[:, :, 0:32], r13, r23))
                    ph.step(block, "vector", lambda v, s3=s3, r13=r13, cosb=cosb: v.tensor_tensor(
                        r13, s3[:, :, 1:64:2], cosb, op=ALU.mult))
                    ph.step(block, "vector", lambda v, s3=s3, r23=r23, sinb=sinb: v.tensor_tensor(
                        r23, s3[:, :, 0:64:2], sinb, op=ALU.mult))
                    ph.step(block, "vector", lambda v, d3=d3, r13=r13, r23=r23: v.tensor_add(
                        d3[:, :, 32:64], r13, r23))
                transpose_set(ph, block, lambda j: roq[:, j * 128:(j + 1) * 128],
                              lambda j, tt2=tt2: q_fm[:, j * 256 + tt2 * 128:j * 256 + tt2 * 128 + 128],
                              8, ptr, ident[:])
                transpose_set(ph, block, lambda j: rok[:, j * 128:(j + 1) * 128],
                              lambda j, tt2=tt2: k_fm[:, j * 256 + tt2 * 128:j * 256 + tt2 * 128 + 128],
                              8, ptr, ident[:])
            for h in range(16):
                base = (h % 2) * 64
                mt = h // 2
                def mmsc(t, base=base, mt=mt):
                    t.matmul(psc0[:], k_fm[base:base + 64, mt * 256:mt * 256 + 128],
                             q_fm[base:base + 64, mt * 256:mt * 256 + 256], start=True, stop=True)
                    return t.matmul(psc1[:], k_fm[base:base + 64, mt * 256 + 128:mt * 256 + 256],
                                    q_fm[base:base + 64, mt * 256:mt * 256 + 256], start=True, stop=True)
                ph.step(block, "tensor", mmsc)
                for st in range(2):
                    psc = psc0 if st == 0 else psc1
                    ph.step(block, "vector", lambda v, st=st, psc=psc: v.tensor_add(
                        scf[:], psc[:], msk_sb[:, st * 256:(st + 1) * 256]))
                    ph.step(block, "scalar", lambda s, st=st: s.activation(
                        probs[:, st * 256:(st + 1) * 256], scf[:], AF.Exp, scale=0.125))
                def mmcs(t):
                    t.matmul(pcs[:], onc_bf[:], probs[:, 0:256], start=True, stop=False)
                    return t.matmul(pcs[:], onc_bf[:], probs[:, 256:512], start=False, stop=True)
                ph.step(block, "tensor", mmcs)
                def mmov(t, h=h):
                    t.matmul(po[:], v_bf[:, h * 64:h * 64 + 64], probs[:, 0:256], start=True, stop=False)
                    return t.matmul(po[:], v_bf[:, D + h * 64:D + h * 64 + 64], probs[:, 256:512],
                                    start=False, stop=True)
                ph.step(block, "tensor", mmov)
                ph.step(block, "vector", lambda v: v.reciprocal(cs_f[:], pcs[:]))
                ph.step(block, "scalar", lambda s: s.copy(cs_b[:], cs_f[:]))
                ph.step(block, "scalar", lambda s, base=base, mt=mt: s.copy(
                    o_fm[base:base + 64, mt * 256:(mt + 1) * 256], po[:]))
                ph.step(block, "tensor", lambda t: t.matmul(
                    po[:], onk_bf[:, 0:64], cs_b[:], start=True, stop=True))
                ph.step(block, "vector", lambda v, base=base, mt=mt: v.tensor_mul(
                    o_fm2[base:base + 64, mt * 256:(mt + 1) * 256],
                    o_fm[base:base + 64, mt * 256:(mt + 1) * 256], po[:]))
            for tt2 in range(2):
                i = blk * 2 + tt2
                for jb in range(2):
                    pp = p5a if jb == 0 else p5b
                    def mmo(t, tt2=tt2, jb=jb, pp=pp):
                        for k in range(8):
                            ins = t.matmul(pp[:], o_fm2[:, k * 256 + tt2 * 128:k * 256 + tt2 * 128 + 128],
                                           woutT[:, k * D + jb * 512:k * D + jb * 512 + 512],
                                           start=(k == 0), stop=(k == 7))
                        return ins
                    ph.step(block, "tensor", mmo)
                    ph.step(block, "vector", lambda v, tt2=tt2, jb=jb, pp=pp: v.tensor_add(
                        res[:, jb * 512:(jb + 1) * 512],
                        x_g[:, tt2 * D + jb * 512:tt2 * D + jb * 512 + 512], pp[:]),
                        waits=[store_mark] if (store_mark is not None and jb == 0 and tt2 == 0) else [])
                ph.dma(block, "sync", lambda e, Dm, DT, DI, i=i: Dm(
                    xout[i * 128:(i + 1) * 128, :], res[:]))
                store_mark = ph.qv("sync")
    with nc.Block(nm + "_fin") as block:
        ph.step(block, "sync", lambda e: e.nop(), waits=[ph.qv("sync")])
    return ph.done()


def phase_aa1(prog, ph, xin, xout, w):
    nc = prog.nc
    nm = ph.name
    wqkvT_bf = prog.dram(nm + "_wqkvT", [D, 3072], BF16)
    wprojT_bf = prog.dram(nm + "_wprojT", [D, D], BF16)
    with nc.Block(nm + "_wc") as block:
        ph.dma(block, "gp", lambda e, Dm, DT, DI: (
            Dm(wqkvT_bf, w["WqkvT"]), Dm(wprojT_bf, w["WprojT"]))[-1])
    cast_q = ph.qv("gp")
    wqkvT = prog.sbuf(nm + "_wq", [128, 8 * 3072], BF16)
    wprojT = prog.sbuf(nm + "_wp", [128, 8 * D], BF16)
    with nc.Block(nm + "_wl") as block:
        def wl(e, Dm, DT, DI):
            for p in range(8):
                Dm(wqkvT[:, p * 3072:(p + 1) * 3072], wqkvT_bf[p * 128:(p + 1) * 128, :])
                ins = Dm(wprojT[:, p * D:(p + 1) * D], wprojT_bf[p * 128:(p + 1) * 128, :])
            return ins
        ph.dma(block, "sync", wl, waits=[cast_q])
    wq_ldq = ph.qv("sync")
    ident = prog.sbuf(nm + "_id", [128, 128], BF16)
    msk_sb = prog.sbuf(nm + "_msk", [128, 4 * 512], BF16)
    onc_bf = prog.sbuf(nm + "_onc", [128, 1], BF16)
    onk_bf = prog.sbuf(nm + "_onk", [1, 128], BF16)
    eps_col = prog.sbuf(nm + "_eps", [128, 1], F32)
    with nc.Block(nm + "_cl") as block:
        ph.dma(block, "sync", lambda e, Dm, DT, DI: (
            Dm(ident[:], w["ident_bf"]), Dm(msk_sb[:], w["diag_mask"]),
            Dm(onc_bf[:], w["ones_col_bf"]), Dm(onk_bf[:], w["ones_k1_bf"]))[-1])
        ph.step(block, "vector", lambda v: v.memset(eps_col[:], 1e-5))

    x_sb = prog.sbuf(nm + "_x", [128, D], F32)
    xm = prog.sbuf(nm + "_xm", [128, D], F32)
    sq = prog.sbuf(nm + "_sq", [128, D], F32)
    stat = prog.sbuf(nm + "_st", [128, 4], F32)
    ln_bf = prog.sbuf(nm + "_ln", [128, D], BF16)
    lnT_all = prog.sbuf(nm + "_lnT", [128, 8 * T // 1], BF16)
    k_fm = prog.sbuf(nm + "_kfm", [128, 8 * 2048], BF16)
    v_tm = prog.sbuf(nm + "_vtm", [128, 16 * D], BF16)
    q_bf = prog.sbuf(nm + "_q", [128, 8 * 512], BF16)
    probs = prog.sbuf(nm + "_pr", [128, 512], BF16)
    probs_e = prog.sbuf(nm + "_pre", [128, 512], BF16)
    cs_f = prog.sbuf(nm + "_csf", [1, 512], F32)
    cs_b = prog.sbuf(nm + "_csb", [1, 512], BF16)
    o_fm = prog.sbuf(nm + "_ofm", [128, 8 * 512], BF16)
    o_fm2 = o_fm
    res = prog.sbuf(nm + "_res", [128, D], F32)

    psc0 = prog.psum(nm + "_ps0", [128, 512], F32)
    psc1 = prog.psum(nm + "_ps1", [128, 512], F32)
    pcs = prog.psum(nm + "_pcs", [1, 512], F32)
    po = prog.psum(nm + "_po", [64, 512], F32)
    p5a = prog.psum(nm + "_p5a", [128, 512], F32)
    ptr = [prog.psum(nm + "_pt0", [128, 128], BF16), prog.psum(nm + "_pt1", [128, 128], BF16)]

    store_mark = None
    for b in range(BLOC):
        with nc.Block(f"{nm}_pre{b}") as block:
            for tt in range(TPB):
                i = b * TPB + tt
                ph.dma(block, "sync", lambda e, Dm, DT, DI, i=i: Dm(
                    x_sb[:], xin[i * 128:(i + 1) * 128, :]))
                lw = ph.qv("sync")
                ln_steps2(ph, block, x_sb[:], xm, sq, ln_bf[:], stat, eps_col[:], waits=[lw])
                transpose_set(ph, block, lambda j: ln_bf[:, j * 128:(j + 1) * 128],
                              lambda j, tt=tt: lnT_all[:, j * 2048 + tt * 128:j * 2048 + tt * 128 + 128],
                              8, ptr, ident[:])
        with nc.Block(f"{nm}_kv{b}") as block:
            for mt in range(8):
                for tg in range(4):
                    def mmk(t, mt=mt, tg=tg):
                        for k in range(8):
                            ins = t.matmul(psc0[:], wqkvT[:, k * 3072 + 1024 + mt * 128:k * 3072 + 1024 + mt * 128 + 128],
                                           lnT_all[:, k * 2048 + tg * 512:k * 2048 + tg * 512 + 512],
                                           start=(k == 0), stop=(k == 7))
                        return ins
                    ph.step(block, "tensor", mmk, waits=[wq_ldq] if (b == 0 and mt == 0 and tg == 0) else [])
                    ph.step(block, "scalar", lambda s, mt=mt, tg=tg: s.copy(
                        k_fm[:, mt * 2048 + tg * 512:mt * 2048 + tg * 512 + 512], psc0[:]))
            for tt in range(TPB):
                for jb in range(2):
                    def mmv(t, tt=tt, jb=jb):
                        for k in range(8):
                            ins = t.matmul(psc1[:], lnT_all[:, k * 2048 + tt * 128:k * 2048 + tt * 128 + 128],
                                           wqkvT[:, k * 3072 + 2048 + jb * 512:k * 3072 + 2048 + jb * 512 + 512],
                                           start=(k == 0), stop=(k == 7))
                        return ins
                    ph.step(block, "tensor", mmv)
                    ph.step(block, "scalar", lambda s, tt=tt, jb=jb: s.copy(
                        v_tm[:, tt * D + jb * 512:tt * D + jb * 512 + 512], psc1[:]))
        for tg in range(4):
            with nc.Block(f"{nm}_tg{b}_{tg}") as block:
                for mt in range(8):
                    def mmq(t, mt=mt, tg=tg):
                        for k in range(8):
                            ins = t.matmul(psc0[:], wqkvT[:, k * 3072 + mt * 128:k * 3072 + mt * 128 + 128],
                                           lnT_all[:, k * 2048 + tg * 512:k * 2048 + tg * 512 + 512],
                                           start=(k == 0), stop=(k == 7))
                        return ins
                    ph.step(block, "tensor", mmq)
                    ph.step(block, "scalar", lambda s, mt=mt: s.copy(
                        q_bf[:, mt * 512:(mt + 1) * 512], psc0[:]))
                nst = 4 * (tg + 1)
                for h in range(16):
                    base = (h % 2) * 64
                    mt = h // 2
                    for st in range(nst):
                        psc = psc0 if st % 2 == 0 else psc1
                        ph.step(block, "tensor", lambda t, st=st, base=base, mt=mt, psc=psc: t.matmul(
                            psc[:], k_fm[base:base + 64, mt * 2048 + st * 128:mt * 2048 + st * 128 + 128],
                            q_bf[base:base + 64, mt * 512:(mt + 1) * 512], start=True, stop=True))
                        r = st - 4 * tg
                        if r >= 0:
                            ph.step(block, "scalar", lambda s, psc=psc: s.activation(
                                probs_e[:], psc[:], AF.Exp, scale=0.125))
                            ph.step(block, "vector", lambda v, r=r: v.tensor_mul(
                                probs[:], probs_e[:], msk_sb[:, r * 512:(r + 1) * 512]))
                        else:
                            ph.step(block, "scalar", lambda s, psc=psc: s.activation(
                                probs[:], psc[:], AF.Exp, scale=0.125))
                        ph.step(block, "tensor", lambda t, st=st, nst=nst: t.matmul(
                            pcs[:], onc_bf[:], probs[:], start=(st == 0), stop=(st == nst - 1)))
                        ph.step(block, "tensor", lambda t, st=st, h=h, nst=nst: t.matmul(
                            po[:], v_tm[:, st * D + h * 64:st * D + h * 64 + 64], probs[:],
                            start=(st == 0), stop=(st == nst - 1)))
                    ph.step(block, "vector", lambda v: v.reciprocal(cs_f[:], pcs[:]))
                    ph.step(block, "scalar", lambda s: s.copy(cs_b[:], cs_f[:]))
                    ph.step(block, "scalar", lambda s, base=base, mt=mt: s.copy(
                        o_fm[base:base + 64, mt * 512:(mt + 1) * 512], po[:]))
                    ph.step(block, "tensor", lambda t: t.matmul(
                        po[:], onk_bf[:, 0:64], cs_b[:], start=True, stop=True))
                    ph.step(block, "vector", lambda v, base=base, mt=mt: v.tensor_mul(
                        o_fm2[base:base + 64, mt * 512:(mt + 1) * 512],
                        o_fm[base:base + 64, mt * 512:(mt + 1) * 512], po[:]))
                for tt2 in range(4):
                    i = b * TPB + tg * 4 + tt2
                    ph.dma(block, "sync", lambda e, Dm, DT, DI, i=i: Dm(
                        x_sb[:], xin[i * 128:(i + 1) * 128, :]))
                    xw = ph.qv("sync")
                    for jb in range(2):
                        def mmo(t, tt2=tt2, jb=jb):
                            for k in range(8):
                                ins = t.matmul(p5a[:], o_fm2[:, k * 512 + tt2 * 128:k * 512 + tt2 * 128 + 128],
                                               wprojT[:, k * D + jb * 512:k * D + jb * 512 + 512],
                                               start=(k == 0), stop=(k == 7))
                            return ins
                        ph.step(block, "tensor", mmo)
                        wts = [xw] if jb == 0 else []
                        if store_mark is not None and jb == 0:
                            wts.append(store_mark)
                        ph.step(block, "vector", lambda v, jb=jb: v.tensor_add(
                            res[:, jb * 512:(jb + 1) * 512], x_sb[:, jb * 512:(jb + 1) * 512], p5a[:]),
                            waits=wts)
                    ph.dma(block, "sync", lambda e, Dm, DT, DI, i=i: Dm(
                        xout[i * 128:(i + 1) * 128, :], res[:]))
                    store_mark = ph.qv("sync")
    with nc.Block(nm + "_fin") as block:
        ph.step(block, "sync", lambda e: e.nop(), waits=[ph.qv("sync")])
    return ph.done()


def phase_ab1(prog, ph, xin, xout, w):
    nc = prog.nc
    nm = ph.name
    wqkvT_bf = prog.dram(nm + "_wqkvT", [D, 3072], BF16)
    woutT_bf = prog.dram(nm + "_woutT", [D, D], BF16)
    with nc.Block(nm + "_wc") as block:
        ph.dma(block, "gp", lambda e, Dm, DT, DI: (
            Dm(wqkvT_bf, w["WqkvT"]), Dm(woutT_bf, w["WoutT"]))[-1])
    cast_q = ph.qv("gp")
    qn_hbm = prog.dram(nm + "_qn", [NTOK, D], BF16)
    wqkvT = prog.sbuf(nm + "_wq", [128, 8 * 3072], BF16)
    woutT = prog.sbuf(nm + "_wo", [128, 8 * D], BF16)
    with nc.Block(nm + "_wl") as block:
        def wl(e, Dm, DT, DI):
            for p in range(8):
                Dm(wqkvT[:, p * 3072:(p + 1) * 3072], wqkvT_bf[p * 128:(p + 1) * 128, :])
                ins = Dm(woutT[:, p * D:(p + 1) * D], woutT_bf[p * 128:(p + 1) * 128, :])
            return ins
        ph.dma(block, "sync", wl, waits=[cast_q])
    wq_ldq = ph.qv("sync")
    ident = prog.sbuf(nm + "_id", [128, 128], BF16)
    cos_sb = prog.sbuf(nm + "_cos", [128, 512], F32)
    sin_sb = prog.sbuf(nm + "_sin", [128, 512], F32)
    qb_rep = prog.sbuf(nm + "_qb", [128, 3072], BF16)
    ob_rep = prog.sbuf(nm + "_ob", [128, D], BF16)
    onc_bf = prog.sbuf(nm + "_onc", [128, 1], BF16)
    onk_bf = prog.sbuf(nm + "_onk", [1, 128], BF16)
    eps_col = prog.sbuf(nm + "_eps", [128, 1], F32)
    with nc.Block(nm + "_cl") as block:
        ph.dma(block, "gp", lambda e, Dm, DT, DI: (
            Dm(qb_rep[:], w["qkvb_rep"]), Dm(ob_rep[:], w["outb_rep"]))[-1])
        ph.dma(block, "sync", lambda e, Dm, DT, DI: (
            Dm(ident[:], w["ident_bf"]), Dm(cos_sb[:], w["cos_tm"]),
            Dm(sin_sb[:], w["sin_tm"]),
            Dm(onc_bf[:], w["ones_col_bf"]), Dm(onk_bf[:], w["ones_k1_bf"]))[-1])
        ph.step(block, "vector", lambda v: v.memset(eps_col[:], 1e-5))
    cl_gp = ph.qv("gp")

    x_sb = prog.sbuf(nm + "_x", [128, D], F32)
    xm = prog.sbuf(nm + "_xm", [128, D], F32)
    sq = prog.sbuf(nm + "_sq", [128, D], F32)
    stat = prog.sbuf(nm + "_st", [128, 4], F32)
    ln_bf = prog.sbuf(nm + "_ln", [128, D], BF16)
    lnT = prog.sbuf(nm + "_lnT", [128, 8 * 128], BF16)
    qkf = prog.sbuf(nm + "_qkf", [128, 2048], F32)
    v_tm = prog.sbuf(nm + "_vtm", [128, 16 * D], BF16)
    rp1 = prog.sbuf(nm + "_rp1", [128, 512], F32)
    rp2 = prog.sbuf(nm + "_rp2", [128, 512], F32)
    roq = prog.sbuf(nm + "_roq", [128, D], BF16)
    rok = prog.sbuf(nm + "_rok", [128, D], BF16)
    nrm = prog.sbuf(nm + "_nrm", [128, 48], F32)
    qn_bf = prog.sbuf(nm + "_qnb", [128, D], BF16)
    kn_bf = prog.sbuf(nm + "_kn", [128, D], BF16)
    k_fm = prog.sbuf(nm + "_kfm", [128, 8 * 2048], BF16)
    q_fm = prog.sbuf(nm + "_qfm", [128, 8 * 512], BF16)
    qt_sb = prog.sbuf(nm + "_qt", [128, D], BF16)
    probs = prog.sbuf(nm + "_pr", [128, 512], BF16)
    cs_f = prog.sbuf(nm + "_csf", [1, 512], F32)
    cs_b = prog.sbuf(nm + "_csb", [1, 512], BF16)
    o_fm = prog.sbuf(nm + "_ofm", [128, 8 * 512], BF16)
    ores = prog.sbuf(nm + "_ors", [128, D], F32)
    res = prog.sbuf(nm + "_res", [128, D], F32)

    psc0 = prog.psum(nm + "_ps0", [128, 512], F32)
    psc1 = prog.psum(nm + "_ps1", [128, 512], F32)
    pcs = prog.psum(nm + "_pcs", [1, 512], F32)
    po = prog.psum(nm + "_po", [64, 512], F32)
    p5a = prog.psum(nm + "_p5a", [128, 512], F32)
    ptr = [prog.psum(nm + "_pt0", [128, 128], BF16), prog.psum(nm + "_pt1", [128, 128], BF16)]

    store_mark = None
    for b in range(BLOC):
        with nc.Block(f"{nm}_pre{b}") as block:
            for tt in range(TPB):
                i = b * TPB + tt
                ph.dma(block, "sync", lambda e, Dm, DT, DI, i=i: Dm(
                    x_sb[:], xin[i * 128:(i + 1) * 128, :]))
                lw = ph.qv("sync")
                ln_steps2(ph, block, x_sb[:], xm, sq, ln_bf[:], stat, eps_col[:], waits=[lw])
                transpose_set(ph, block, lambda j: ln_bf[:, j * 128:(j + 1) * 128],
                              lambda j: lnT[:, j * 128:(j + 1) * 128], 8, ptr, ident[:])
                for nt6 in range(6):
                    pp = psc0 if nt6 % 2 == 0 else psc1
                    def mmp(t, nt6=nt6, pp=pp):
                        for k in range(8):
                            ins = t.matmul(pp[:], lnT[:, k * 128:(k + 1) * 128],
                                           wqkvT[:, k * 3072 + nt6 * 512:k * 3072 + nt6 * 512 + 512],
                                           start=(k == 0), stop=(k == 7))
                        return ins
                    wts = []
                    if b == 0 and tt == 0 and nt6 == 0:
                        wts = [wq_ldq, cl_gp]
                    ph.step(block, "tensor", mmp, waits=wts)
                    if nt6 < 4:
                        ph.step(block, "vector", lambda v, nt6=nt6, pp=pp: v.tensor_add(
                            qkf[:, nt6 * 512:(nt6 + 1) * 512], pp[:],
                            qb_rep[:, nt6 * 512:(nt6 + 1) * 512]))
                    else:
                        ph.step(block, "vector", lambda v, nt6=nt6, pp=pp, tt=tt: v.tensor_add(
                            v_tm[:, tt * D + (nt6 - 4) * 512:tt * D + (nt6 - 3) * 512], pp[:],
                            qb_rep[:, 2048 + (nt6 - 4) * 512:2048 + (nt6 - 3) * 512]))
                cosb = cos_sb[:, tt * 32:(tt + 1) * 32].unsqueeze(1).broadcast_to([128, 16, 32])
                sinb = sin_sb[:, tt * 32:(tt + 1) * 32].unsqueeze(1).broadcast_to([128, 16, 32])
                for (src0, rodst, ndst) in ((0, roq, qn_bf), (1, rok, kn_bf)):
                    s3 = qkf[:, src0 * 1024:(src0 + 1) * 1024].rearrange("p (h d) -> p h d", h=16)
                    d3 = rodst[:].rearrange("p (h d) -> p h d", h=16)
                    r13 = rp1[:].rearrange("p (h d) -> p h d", h=16)
                    r23 = rp2[:].rearrange("p (h d) -> p h d", h=16)
                    ph.step(block, "vector", lambda v, s3=s3, r13=r13, cosb=cosb: v.tensor_tensor(
                        r13, s3[:, :, 0:64:2], cosb, op=ALU.mult))
                    ph.step(block, "vector", lambda v, s3=s3, r23=r23, sinb=sinb: v.tensor_tensor(
                        r23, s3[:, :, 1:64:2], sinb, op=ALU.mult))
                    ph.step(block, "vector", lambda v, d3=d3, r13=r13, r23=r23: v.tensor_sub(
                        d3[:, :, 0:64:2], r13, r23))
                    ph.step(block, "vector", lambda v, s3=s3, r13=r13, cosb=cosb: v.tensor_tensor(
                        r13, s3[:, :, 1:64:2], cosb, op=ALU.mult))
                    ph.step(block, "vector", lambda v, s3=s3, r23=r23, sinb=sinb: v.tensor_tensor(
                        r23, s3[:, :, 0:64:2], sinb, op=ALU.mult))
                    ph.step(block, "vector", lambda v, d3=d3, r13=r13, r23=r23: v.tensor_add(
                        d3[:, :, 1:64:2], r13, r23))
                    ph.step(block, "scalar", lambda s, rodst=rodst: s.activation(
                        sq[:], rodst[:], AF.Square))
                    ph.step(block, "vector", lambda v: v.tensor_reduce(
                        nrm[:, 0:16], sq[:].rearrange("p (h d) -> p h d", h=16), axis=AX.X, op=ALU.add))
                    ph.step(block, "scalar", lambda s: s.activation(nrm[:, 16:32], nrm[:, 0:16], AF.Sqrt))
                    ph.step(block, "vector", lambda v: v.tensor_scalar_max(nrm[:, 0:16], nrm[:, 16:32], 1e-12))
                    ph.step(block, "vector", lambda v: v.reciprocal(nrm[:, 32:48], nrm[:, 0:16]))
                    ph.step(block, "vector", lambda v, d3=d3, ndst=ndst: v.tensor_tensor(
                        ndst[:].rearrange("p (h d) -> p h d", h=16), d3,
                        nrm[:, 32:48].unsqueeze(2).broadcast_to([128, 16, 64]), op=ALU.mult))
                ph.dma(block, "sync", lambda e, Dm, DT, DI, i=i: Dm(
                    qn_hbm[i * 128:(i + 1) * 128, :], qn_bf[:]))
                transpose_set(ph, block, lambda j: kn_bf[:, j * 128:(j + 1) * 128],
                              lambda j, tt=tt: k_fm[:, j * 2048 + tt * 128:j * 2048 + tt * 128 + 128],
                              8, ptr, ident[:])
        for tg in range(4):
            with nc.Block(f"{nm}_tg{b}_{tg}") as block:
                for tt2 in range(4):
                    i = b * TPB + tg * 4 + tt2
                    ph.dma(block, "sync", lambda e, Dm, DT, DI, i=i: Dm(
                        qt_sb[:], qn_hbm[i * 128:(i + 1) * 128, :]))
                    qw = ph.qv("sync")
                    transpose_set(ph, block, lambda j: qt_sb[:, j * 128:(j + 1) * 128],
                                  lambda j, tt2=tt2: q_fm[:, j * 512 + tt2 * 128:j * 512 + tt2 * 128 + 128],
                                  8, ptr, ident[:], waits=[qw])
                for h in range(16):
                    base = (h % 2) * 64
                    mt = h // 2
                    for st in range(16):
                        psc = psc0 if st % 2 == 0 else psc1
                        ph.step(block, "tensor", lambda t, st=st, base=base, mt=mt, psc=psc: t.matmul(
                            psc[:], k_fm[base:base + 64, mt * 2048 + st * 128:mt * 2048 + st * 128 + 128],
                            q_fm[base:base + 64, mt * 512:(mt + 1) * 512],
                            start=True, stop=True))
                        ph.step(block, "scalar", lambda s, psc=psc: s.activation(
                            probs[:], psc[:], AF.Exp, scale=0.125))
                        ph.step(block, "tensor", lambda t, st=st: t.matmul(
                            pcs[:], onc_bf[:], probs[:], start=(st == 0), stop=(st == 15)))
                        ph.step(block, "tensor", lambda t, st=st, h=h: t.matmul(
                            po[:], v_tm[:, st * D + h * 64:st * D + h * 64 + 64], probs[:],
                            start=(st == 0), stop=(st == 15)))
                    ph.step(block, "vector", lambda v: v.reciprocal(cs_f[:], pcs[:]))
                    ph.step(block, "scalar", lambda s: s.copy(cs_b[:], cs_f[:]))
                    ph.step(block, "scalar", lambda s, base=base, mt=mt: s.copy(
                        o_fm[base:base + 64, mt * 512:(mt + 1) * 512], po[:]))
                    ph.step(block, "tensor", lambda t: t.matmul(
                        po[:], onk_bf[:, 0:64], cs_b[:], start=True, stop=True))
                    ph.step(block, "vector", lambda v, base=base, mt=mt: v.tensor_mul(
                        o_fm[base:base + 64, mt * 512:(mt + 1) * 512],
                        o_fm[base:base + 64, mt * 512:(mt + 1) * 512], po[:]))
                for tt2 in range(4):
                    i = b * TPB + tg * 4 + tt2
                    ph.dma(block, "sync", lambda e, Dm, DT, DI, i=i: Dm(
                        x_sb[:], xin[i * 128:(i + 1) * 128, :]))
                    xw = ph.qv("sync")
                    for jb in range(2):
                        def mmo(t, tt2=tt2, jb=jb):
                            for k in range(8):
                                ins = t.matmul(p5a[:], o_fm[:, k * 512 + tt2 * 128:k * 512 + tt2 * 128 + 128],
                                               woutT[:, k * D + jb * 512:k * D + jb * 512 + 512],
                                               start=(k == 0), stop=(k == 7))
                            return ins
                        ph.step(block, "tensor", mmo)
                        ph.step(block, "vector", lambda v, jb=jb: v.tensor_add(
                            ores[:, jb * 512:(jb + 1) * 512], p5a[:], ob_rep[:, jb * 512:(jb + 1) * 512]))
                        wts = [xw] if jb == 0 else []
                        if store_mark is not None and jb == 0:
                            wts.append(store_mark)
                        ph.step(block, "vector", lambda v, jb=jb: v.tensor_add(
                            res[:, jb * 512:(jb + 1) * 512], x_sb[:, jb * 512:(jb + 1) * 512],
                            ores[:, jb * 512:(jb + 1) * 512]), waits=wts)
                    ph.dma(block, "sync", lambda e, Dm, DT, DI, i=i: Dm(
                        xout[i * 128:(i + 1) * 128, :], res[:]))
                    store_mark = ph.qv("sync")
    with nc.Block(nm + "_fin") as block:
        ph.step(block, "sync", lambda e: e.nop(), waits=[ph.qv("sync")])
    return ph.done()



CFIX = 1536
NSLOT = 8 * CFIX

def phase_moe(prog, ph, xin, xout, w):
    nc = prog.nc
    nm = ph.name
    DFF = 4096
    preT_bf = w["preT_bf"]; gateT_bf = w["gateT_bf"]; postT_bf = w["postT_bf"]; wrT_bf = w["wrT_bf"]
    if w["do_cast"]:
        with nc.Block(nm + "_wc") as block:
            def wc(e, Dm, DT, DI):
                for eix in range(8):
                    Dm(preT_bf[eix * D:(eix + 1) * D, :], w["W_preT"][eix])
                    Dm(gateT_bf[eix * D:(eix + 1) * D, :], w["W_gateT"][eix])
                    ins = Dm(postT_bf[eix * DFF:(eix + 1) * DFF, :], w["W_postT"][eix])
                ins = Dm(wrT_bf, w["W_rT"])
                return ins
            ph.dma(block, "gp", wc)
    cast_q = ph.qv("gp")

    ln_hbm = prog.dram(nm + "_ln", [NTOK + 1, D], BF16)
    slab = prog.dram(nm + "_slab", [NSLOT + 1, 3], I32)
    P_buf = prog.dram(nm + "_P", [2 * NTOK + 1, D], F32)

    ident = prog.sbuf(nm + "_id", [128, 128], BF16)
    eps_col = prog.sbuf(nm + "_eps", [128, 1], F32)
    wrT_sb = prog.sbuf(nm + "_wr", [128, 8 * 8], BF16)
    with nc.Block(nm + "_cl") as block:
        ph.dma(block, "gp", lambda e, Dm, DT, DI: Dm(slab, w["slab_init"]))
        def cl(e, Dm, DT, DI):
            ins = Dm(ident[:], w["ident_bf"])
            for p in range(8):
                ins = Dm(wrT_sb[:, p * 8:(p + 1) * 8], wrT_bf[p * 128:(p + 1) * 128, :])
            return ins
        ph.dma(block, "sync", cl, waits=[cast_q])
        ph.step(block, "vector", lambda v: v.memset(eps_col[:], 1e-5))
    slab_q = ph.qv("gp")
    wr_ldq = ph.qv("sync")

    with prog.scope():
        x_sb = prog.sbuf(nm + "_x", [128, D], F32)
        xm = prog.sbuf(nm + "_xm", [128, D], F32)
        sq = prog.sbuf(nm + "_sq", [128, D], F32)
        stat = prog.sbuf(nm + "_st", [128, 4], F32)
        ln_bf = prog.sbuf(nm + "_lnb", [128, D], BF16)
        lnT = prog.sbuf(nm + "_lnT", [128, 8 * 128], BF16)
        zrow = prog.sbuf(nm + "_z", [1, D], BF16)
        probs = prog.sbuf(nm + "_pb", [128, 8], F32)
        mk1 = prog.sbuf(nm + "_mk1", [128, 8], F32)
        mk2 = prog.sbuf(nm + "_mk2", [128, 8], F32)
        om = prog.sbuf(nm + "_om", [128, 8], F32)
        p2 = prog.sbuf(nm + "_p2", [128, 8], F32)
        t8 = prog.sbuf(nm + "_t8", [128, 8], F32)
        iota8 = prog.sbuf(nm + "_io8", [128, 8], F32)
        onescol = prog.sbuf(nm + "_onc", [128, 1], F32)
        onesk1 = prog.sbuf(nm + "_onk", [1, 128], F32)
        Lstr = prog.sbuf(nm + "_Ls", [128, 128], F32)
        cols = prog.sbuf(nm + "_co", [128, 16], F32)
        base_sb = prog.sbuf(nm + "_ba", [1, 16], F32)
        cs_sb = prog.sbuf(nm + "_cs", [1, 24], F32)
        s1f = prog.sbuf(nm + "_s1f", [128, 8], F32)
        s2f = prog.sbuf(nm + "_s2f", [128, 8], F32)
        toki = prog.sbuf(nm + "_tok", [128, 32], I32)
        tok2a = prog.sbuf(nm + "_t2a", [128, 32], I32)
        tok2b = prog.sbuf(nm + "_t2b", [128, 32], I32)
        packed = prog.sbuf(nm + "_pk", [128, 3], I32)
        packed2 = prog.sbuf(nm + "_pk2", [128, 3], I32)
        dsti = prog.sbuf(nm + "_di", [128, 2], I32)

        pl = prog.psum(nm + "_pl", [128, 8], F32)
        ppf1 = prog.psum(nm + "_pp1", [128, 8], F32)
        ppf2 = prog.psum(nm + "_pp2", [128, 8], F32)
        pco = prog.psum(nm + "_pco", [1, 8], F32)
        pc1 = prog.psum(nm + "_pc1", [1, 8], F32)
        pbs = prog.psum(nm + "_pbs", [128, 8], F32)
        ptr = [prog.psum(nm + "_pt0", [128, 128], BF16), prog.psum(nm + "_pt1", [128, 128], BF16)]

        with nc.Block(nm + "_ri") as block:
            ph.dma(block, "sync", lambda e, Dm, DT, DI: (
                Dm(iota8[:], w["iota8"]), Dm(onescol[:], w["ones_col"]),
                Dm(onesk1[:], w["ones_k1"]), Dm(Lstr[:], w["Lstrict"]),
                Dm(toki[:], w["tok_i32"]), Dm(tok2a[:], w["tok2a_i32"]),
                Dm(tok2b[:], w["tok2b_i32"]))[-1])
            ph.step(block, "vector", lambda v: v.memset(zrow[:], 0.0))
            ph.step(block, "vector", lambda v: v.memset(base_sb[:], 0.0))
            ph.dma(block, "sync", lambda e, Dm, DT, DI: Dm(ln_hbm[NTOK:NTOK + 1, :], zrow[:]))

        for i in range(NT):
            par = i % 2
            with nc.Block(f"{nm}_r{i}") as block:
                ph.dma(block, "sync", lambda e, Dm, DT, DI, i=i: Dm(
                    x_sb[:], xin[i * 128:(i + 1) * 128, :]))
                lw = ph.qv("sync")
                ln_steps2(ph, block, x_sb[:], xm, sq, ln_bf[:], stat, eps_col[:], waits=[lw])
                ph.dma(block, "sync", lambda e, Dm, DT, DI, i=i: Dm(
                    ln_hbm[i * 128:(i + 1) * 128, :], ln_bf[:]))
                transpose_set(ph, block, lambda j: ln_bf[:, j * 128:(j + 1) * 128],
                              lambda j: lnT[:, j * 128:(j + 1) * 128], 8, ptr, ident[:])
                def mml(t):
                    for k in range(8):
                        ins = t.matmul(pl[:], lnT[:, k * 128:(k + 1) * 128],
                                       wrT_sb[:, k * 8:(k + 1) * 8], start=(k == 0), stop=(k == 7))
                    return ins
                ph.step(block, "tensor", mml, waits=[wr_ldq] if i == 0 else [])
                ph.step(block, "vector", lambda v: v.reduce_max(cols[:, 0:1], pl[:], axis=AX.X))
                ph.step(block, "scalar", lambda s: s.activation(cols[:, 1:2], cols[:, 0:1], AF.Copy, scale=-1.0))
                ph.step(block, "scalar", lambda s: s.activation(
                    probs[:], pl[:], AF.Exp, bias=cols[:, 1:2], accum_out=cols[:, 2:3]))
                ph.step(block, "vector", lambda v: v.reciprocal(cols[:, 3:4], cols[:, 2:3]))
                ph.step(block, "vector", lambda v: v.tensor_scalar_mul(probs[:], probs[:], cols[:, 3:4]))
                ph.step(block, "vector", lambda v: v.reduce_max(cols[:, 4:5], probs[:], axis=AX.X))
                ph.step(block, "vector", lambda v: v.tensor_single_scalar(mk1[:], probs[:], cols[:, 4:5], op=ALU.is_ge))
                ph.step(block, "vector", lambda v: v.tensor_mul(t8[:], mk1[:], iota8[:]))
                ph.step(block, "vector", lambda v: v.reduce_sum(cols[:, 5:6], t8[:], axis=AX.X))
                ph.step(block, "vector", lambda v: v.tensor_scalar(om[:], mk1[:], -1.0, 1.0, op0=ALU.mult, op1=ALU.add))
                ph.step(block, "vector", lambda v: v.tensor_mul(p2[:], probs[:], om[:]))
                ph.step(block, "vector", lambda v: v.reduce_max(cols[:, 6:7], p2[:], axis=AX.X))
                ph.step(block, "vector", lambda v: v.tensor_single_scalar(mk2[:], p2[:], cols[:, 6:7], op=ALU.is_ge))
                ph.step(block, "vector", lambda v: v.tensor_mul(t8[:], mk2[:], iota8[:]))
                ph.step(block, "vector", lambda v: v.reduce_sum(cols[:, 7:8], t8[:], axis=AX.X))
                ph.step(block, "vector", lambda v: v.tensor_add(cols[:, 8:9], cols[:, 4:5], cols[:, 6:7]))
                ph.step(block, "vector", lambda v: v.reciprocal(cols[:, 9:10], cols[:, 8:9]))
                ph.step(block, "vector", lambda v: v.tensor_mul(cols[:, 10:11], cols[:, 4:5], cols[:, 9:10]))
                ph.step(block, "vector", lambda v: v.tensor_mul(cols[:, 11:12], cols[:, 6:7], cols[:, 9:10]))
                def mmpf(t):
                    t.matmul(ppf1[:], Lstr[:], mk1[:], start=True, stop=True)
                    return t.matmul(ppf2[:], Lstr[:], mk2[:], start=True, stop=True)
                ph.step(block, "tensor", mmpf)
                ph.step(block, "tensor", lambda t: t.matmul(pc1[:], onescol[:], mk1[:], start=True, stop=True))
                def mmco(t):
                    t.matmul(pco[:], onescol[:], mk1[:], start=True, stop=False)
                    return t.matmul(pco[:], onescol[:], mk2[:], start=False, stop=True)
                ph.step(block, "tensor", mmco)
                bslice = base_sb[:, (par) * 8:(par) * 8 + 8]
                bnext = base_sb[:, (1 - par) * 8:(1 - par) * 8 + 8]
                ph.step(block, "tensor", lambda t, bslice=bslice: t.matmul(
                    pbs[:], onesk1[:], bslice, start=True, stop=True))
                ph.step(block, "scalar", lambda s: s.copy(s1f[:], ppf1[:]))
                ph.step(block, "vector", lambda v: v.tensor_add(s1f[:], s1f[:], pbs[:]))
                ph.step(block, "vector", lambda v: v.tensor_mul(t8[:], s1f[:], mk1[:]))
                ph.step(block, "vector", lambda v: v.reduce_sum(cols[:, 12:13], t8[:], axis=AX.X))
                ph.step(block, "vector", lambda v: v.tensor_copy(cs_sb[:, 0:8], pc1[:]))
                ph.step(block, "scalar", lambda s: s.copy(s2f[:], ppf2[:]))
                ph.step(block, "vector", lambda v: v.tensor_add(s2f[:], s2f[:], pbs[:]))
                ph.step(block, "tensor", lambda t: t.matmul(ppf1[:], onesk1[:], cs_sb[:, 0:8], start=True, stop=True))
                ph.step(block, "vector", lambda v: v.tensor_add(s2f[:], s2f[:], ppf1[:]))
                ph.step(block, "vector", lambda v: v.tensor_mul(t8[:], s2f[:], mk2[:]))
                ph.step(block, "vector", lambda v: v.reduce_sum(cols[:, 13:14], t8[:], axis=AX.X))
                ph.step(block, "vector", lambda v: v.tensor_copy(cs_sb[:, 8:16], pco[:]))
                ph.step(block, "vector", lambda v, bslice=bslice, bnext=bnext: v.tensor_add(
                    bnext, bslice, cs_sb[:, 8:16]))
                for kk, (ecol, scol, dcol) in enumerate(((5, 12, 0), (7, 13, 1))):
                    ph.step(block, "vector", lambda v, ecol=ecol, scol=scol: v.scalar_tensor_tensor(
                        cols[:, 14:15], cols[:, ecol:ecol + 1], float(CFIX), cols[:, scol:scol + 1],
                        op0=ALU.mult, op1=ALU.add))
                    ph.step(block, "vector", lambda v, scol=scol: v.tensor_single_scalar(
                        cols[:, 15:16], cols[:, scol:scol + 1], float(CFIX), op=ALU.is_ge))
                    ph.step(block, "vector", lambda v: v.scalar_tensor_tensor(
                        cols[:, 14:15], cols[:, 15:16], 1e6, cols[:, 14:15], op0=ALU.mult, op1=ALU.add))
                    ph.step(block, "vector", lambda v: v.tensor_scalar_min(
                        cols[:, 14:15], cols[:, 14:15], float(NSLOT)))
                    ph.step(block, "vector", lambda v, dcol=dcol: v.tensor_copy(
                        dsti[:, dcol:dcol + 1], cols[:, 14:15]))
                def pack1(v, i=i):
                    v.tensor_copy(packed[:, 0:1], toki[:, i:i + 1])
                    v.tensor_copy(packed[:, 1:2], tok2a[:, i:i + 1])
                    return v.tensor_copy(packed[:, 2:3].bitcast(F32), cols[:, 10:11])
                ph.step(block, "vector", pack1)
                def pack2(v, i=i):
                    v.tensor_copy(packed2[:, 0:1], toki[:, i:i + 1])
                    v.tensor_copy(packed2[:, 1:2], tok2b[:, i:i + 1])
                    return v.tensor_copy(packed2[:, 2:3].bitcast(F32), cols[:, 11:12])
                ph.step(block, "vector", pack2)
                ph.dma(block, "gp", lambda e, Dm, DT, DI: (
                    DI(slab[0:128, :], IndirectOffsetOnAxis(dsti[:, 0:1], 0), packed[:], None),
                    DI(slab[0:128, :], IndirectOffsetOnAxis(dsti[:, 1:2], 0), packed2[:], None))[-1],
                    waits=[slab_q] if i == 0 else [])
        route_q = ph.qv("gp")
        route_sync = ph.qv("sync")

    with prog.scope():
        preT_sb = prog.sbuf(nm + "_pre", [128, 8 * 1024], BF16)
        gateT_sb = prog.sbuf(nm + "_gat", [128, 8 * 1024], BF16)
        postT_sb = prog.sbuf(nm + "_pos", [128, 8 * 1024], BF16)
        t3 = prog.sbuf(nm + "_t3", [128, 36], I32)
        h_tok = prog.sbuf(nm + "_ht", [128, D], BF16)
        hT = prog.sbuf(nm + "_hT", [128, 8 * CFIX], BF16)
        h2 = prog.sbuf(nm + "_h2", [128, 8 * 512], BF16)
        gs = prog.sbuf(nm + "_gs", [128, 512], BF16)
        acc = prog.sbuf(nm + "_ac", [128, 12 * 1024], F32)
        outw = prog.sbuf(nm + "_ow", [128, D], F32)
        pA = prog.psum(nm + "_pA", [128, 512], F32)
        pB = prog.psum(nm + "_pB", [128, 512], F32)
        pC = prog.psum(nm + "_pC", [128, 512], F32)
        pD = prog.psum(nm + "_pD", [128, 512], F32)
        ptr2 = [prog.psum(nm + "_pu0", [128, 128], BF16), prog.psum(nm + "_pu1", [128, 128], BF16)]

        for e in range(8):
            with nc.Block(f"{nm}_eg{e}") as block:
                def ldt3(en, Dm, DT, DI, e=e):
                    for gt in range(12):
                        ins = Dm(t3[:, gt * 3:(gt + 1) * 3],
                                 slab[e * CFIX + gt * 128:e * CFIX + (gt + 1) * 128, :])
                    return ins
                ph.dma(block, "sync", ldt3, waits=[route_q, route_sync] if e == 0 else [])
                t3q = ph.qv("sync")
                for gt in range(12):
                    ph.dma(block, "gp", lambda en, Dm, DT, DI, gt=gt: DI(
                        h_tok[:], None, ln_hbm[0:128, :],
                        IndirectOffsetOnAxis(t3[:, gt * 3:gt * 3 + 1], 0)),
                        waits=[t3q] if gt == 0 else [])
                    gq = ph.qv("gp")
                    transpose_set(ph, block,
                                  lambda j: h_tok[:, j * 128:(j + 1) * 128],
                                  lambda j, gt=gt: hT[:, j * CFIX + gt * 128:j * CFIX + gt * 128 + 128],
                                  8, ptr2, ident[:], waits=[gq])
            for q in range(4):
                with nc.Block(f"{nm}_ew{e}_{q}") as block:
                    def ldw(en, Dm, DT, DI, e=e, q=q):
                        for p in range(8):
                            Dm(preT_sb[:, p * 1024:(p + 1) * 1024],
                               preT_bf[e * D + p * 128:e * D + (p + 1) * 128, q * 1024:(q + 1) * 1024])
                            Dm(gateT_sb[:, p * 1024:(p + 1) * 1024],
                               gateT_bf[e * D + p * 128:e * D + (p + 1) * 128, q * 1024:(q + 1) * 1024])
                            ins = Dm(postT_sb[:, p * 1024:(p + 1) * 1024],
                                     postT_bf[e * DFF + q * 1024 + p * 128:e * DFF + q * 1024 + (p + 1) * 128, :])
                        return ins
                    ph.dma(block, "sync", ldw)
                    wq2 = ph.qv("sync")
                    for g in range(3):
                        for mt in range(8):
                            def mmpre(t, q=q, g=g, mt=mt):
                                for k in range(8):
                                    ins = t.matmul(pA[:], preT_sb[:, k * 1024 + mt * 128:k * 1024 + mt * 128 + 128],
                                                   hT[:, k * CFIX + g * 512:k * CFIX + g * 512 + 512],
                                                   start=(k == 0), stop=(k == 7))
                                return ins
                            ph.step(block, "tensor", mmpre, waits=[wq2] if (g == 0 and mt == 0) else [])
                            def mmgat(t, q=q, g=g, mt=mt):
                                for k in range(8):
                                    ins = t.matmul(pB[:], gateT_sb[:, k * 1024 + mt * 128:k * 1024 + mt * 128 + 128],
                                                   hT[:, k * CFIX + g * 512:k * CFIX + g * 512 + 512],
                                                   start=(k == 0), stop=(k == 7))
                                return ins
                            ph.step(block, "tensor", mmgat)
                            ph.step(block, "scalar", lambda s: s.activation(gs[:], pB[:], AF.Silu))
                            ph.step(block, "vector", lambda v, mt=mt: v.tensor_mul(
                                h2[:, mt * 512:(mt + 1) * 512], pA[:], gs[:]))
                        for st in range(4):
                            for jb in range(2):
                                pp = pC if (st * 2 + jb) % 2 == 0 else pD
                                def mmd(t, st=st, jb=jb, pp=pp):
                                    for k in range(8):
                                        ins = t.matmul(pp[:], h2[:, k * 512 + st * 128:k * 512 + st * 128 + 128],
                                                       postT_sb[:, k * 1024 + jb * 512:k * 1024 + jb * 512 + 512],
                                                       start=(k == 0), stop=(k == 7))
                                    return ins
                                ph.step(block, "tensor", mmd)
                                aslice = acc[:, (g * 4 + st) * 1024 + jb * 512:(g * 4 + st) * 1024 + jb * 512 + 512]
                                if q == 0:
                                    ph.step(block, "scalar", lambda s, aslice=aslice, pp=pp: s.copy(aslice, pp[:]))
                                else:
                                    ph.step(block, "vector", lambda v, aslice=aslice, pp=pp: v.tensor_add(
                                        aslice, aslice, pp[:]))
            with nc.Block(f"{nm}_eo{e}") as block:
                for gt in range(12):
                    ph.step(block, "vector", lambda v, gt=gt: v.tensor_single_scalar(
                        outw[:], acc[:, gt * 1024:(gt + 1) * 1024],
                        t3[:, gt * 3 + 2:gt * 3 + 3].bitcast(F32), op=ALU.mult),
                        waits=[ph.qv("gp")])
                    ph.dma(block, "gp", lambda en, Dm, DT, DI, gt=gt: DI(
                        P_buf[0:128, :], IndirectOffsetOnAxis(t3[:, gt * 3 + 1:gt * 3 + 2], 0),
                        outw[:], None))
        exp_q = ph.qv("gp")

    with prog.scope():
        x_sb = prog.sbuf(nm + "_cx", [128, D], F32)
        pe_sb = prog.sbuf(nm + "_pe", [128, D], F32)
        po_sb = prog.sbuf(nm + "_po2", [128, D], F32)
        res = prog.sbuf(nm + "_cres", [128, D], F32)
        for i in range(NT):
            with nc.Block(f"{nm}_c{i}") as block:
                def ldc(en, Dm, DT, DI, i=i):
                    Dm(x_sb[:], xin[i * 128:(i + 1) * 128, :])
                    Dm(pe_sb[:], P_buf[2 * i * 128:2 * (i + 1) * 128:2, :])
                    return Dm(po_sb[:], P_buf[2 * i * 128 + 1:2 * (i + 1) * 128:2, :])
                ph.dma(block, "sync", ldc, waits=[exp_q] if i == 0 else [])
                lw = ph.qv("sync")
                ph.step(block, "vector", lambda v: v.tensor_add(res[:], x_sb[:], pe_sb[:]), waits=[lw])
                ph.step(block, "vector", lambda v: v.tensor_add(res[:], res[:], po_sb[:]))
                ph.dma(block, "sync", lambda en, Dm, DT, DI, i=i: Dm(
                    xout[i * 128:(i + 1) * 128, :], res[:]))
    with nc.Block(nm + "_fin") as block:
        ph.step(block, "sync", lambda e: e.nop(), waits=[ph.qv("sync")])
    return ph.done()


def shard_inputs(inputs):
    x = np.ascontiguousarray(inputs["L0_attn_a_x"], dtype=np.float32)
    return [np.ascontiguousarray(x[c * BLOC:(c + 1) * BLOC].reshape(NTOK, D))
            for c in range(NCORES)]


def fm_cols(vec):
    return np.ascontiguousarray(vec.reshape(8, 128).T.astype(np.float32))


def rep(vec, dtype=np.float32):
    return np.ascontiguousarray(np.broadcast_to(np.asarray(vec, dtype), (128, len(vec))))


_CONSTS = None

def consts():
    global _CONSTS
    if _CONSTS is None:
        _CONSTS = build_consts()
    return _CONSTS


def w_common(prog):
    c = consts()
    return {
        "ident_bf": prog.dram_in_cached("c_ident", lambda: c["ident_bf"]),
        "ident_f32": prog.dram_in_cached("c_identf", lambda: c["ident_f32"]),
    }


def emit_gla(prog, inputs, xin, xout, prev=None):
    c = consts()
    w1 = dict(w_common(prog))
    w1.update({
        "Wq": prog.dram_in_cached("glaWq", lambda: np.ascontiguousarray(inputs["L0_attn_a_Wq"], np.float32)),
        "Wk": prog.dram_in_cached("glaWk", lambda: np.ascontiguousarray(inputs["L0_attn_a_Wk"], np.float32)),
        "Wv": prog.dram_in_cached("glaWv", lambda: np.ascontiguousarray(inputs["L0_attn_a_Wv"], np.float32)),
        "Wgk1": prog.dram_in_cached("glaWgk1", lambda: np.ascontiguousarray(inputs["L0_attn_a_Wgk1"], np.float32)),
        "Wgk2": prog.dram_in_cached("glaWgk2", lambda: np.ascontiguousarray(inputs["L0_attn_a_Wgk2"], np.float32)),
        "gla_U": prog.dram_in_cached("c_glaU", lambda: c["gla_U"]),
        "gla_mask": prog.dram_in_cached("c_glaM", lambda: c["gla_mask"]),
        "rms_rep": prog.dram_in_cached("c_rms", lambda: rep(np.tile(np.asarray(inputs["L0_attn_a_rms_w"], np.float32), 4))),
        "bgk2_rep": prog.dram_in_cached("c_bgk2", lambda: rep(np.asarray(inputs["L0_attn_a_bgk2"], np.float32))),
    })
    w2 = dict(w_common(prog))
    w2.update({
        "Wg": prog.dram_in_cached("glaWg", lambda: np.ascontiguousarray(inputs["L0_attn_a_Wg"], np.float32)),
        "Wo": prog.dram_in_cached("glaWo", lambda: np.ascontiguousarray(inputs["L0_attn_a_Wo"], np.float32)),
    })
    o_hbm = prog.dram("gla_o", [NTOK, 2048], BF16)
    ph1 = Ph(prog, "g1", prev=prev)
    with prog.scope():
        d1 = phase_gla1(prog, ph1, xin, o_hbm, w1)
    ph2 = Ph(prog, "g2", prev=d1)
    with prog.scope():
        return phase_gla2(prog, ph2, xin, o_hbm, xout, w2)


def emit_ffn(prog, inputs, xin, xout, nm, prev=None):
    w = dict(w_common(prog))
    w.update({
        "W_r": prog.dram_in_cached("ffnWr", lambda: np.ascontiguousarray(inputs["L1_ffn_W_r"], np.float32)),
        "W_k": prog.dram_in_cached("ffnWk", lambda: np.ascontiguousarray(inputs["L1_ffn_W_k"], np.float32)),
        "W_v": prog.dram_in_cached("ffnWv", lambda: np.ascontiguousarray(inputs["L1_ffn_W_v"], np.float32)),
        "mu_r_fm": prog.dram_in_cached("ffnmur", lambda: fm_cols(np.asarray(inputs["L1_ffn_mu_r"], np.float32))),
        "mu_k_fm": prog.dram_in_cached("ffnmuk", lambda: fm_cols(np.asarray(inputs["L1_ffn_mu_k"], np.float32))),
    })
    ph = Ph(prog, nm, prev=prev)
    with prog.scope():
        return phase_ffn1(prog, ph, xin, xout, w)


def emit_ab0(prog, inputs, xin, xout, prev=None):
    c = consts()
    w = dict(w_common(prog))
    w.update({
        "WqkvT": prog.dram_in_cached("ab0WqkvT", lambda: np.ascontiguousarray(
            np.asarray(inputs["L0_attn_b_Wqkv"], np.float32).T)),
        "WoutT": prog.dram_in_cached("ab0WoutT", lambda: np.ascontiguousarray(
            np.asarray(inputs["L0_attn_b_Wout"], np.float32).T)),
        "cos_tm": prog.dram_in_cached("c_cos", lambda: c["rope_cos_tm"]),
        "sin_tm": prog.dram_in_cached("c_sin", lambda: c["rope_sin_tm"]),
        "blk_mask": prog.dram_in_cached("c_bmsk", lambda: c["blk_mask_flat"]),
        "ones_col_bf": prog.dram_in_cached("c_ocb", lambda: c["ones_col_bf"]),
        "ones_k1_bf": prog.dram_in_cached("c_okb", lambda: c["ones_k1_bf"]),
    })
    ph = Ph(prog, "b0", prev=prev)
    with prog.scope():
        return phase_ab0(prog, ph, xin, xout, w)


def emit_aa1(prog, inputs, xin, xout, prev=None):
    c = consts()
    w = dict(w_common(prog))
    w.update({
        "WqkvT": prog.dram_in_cached("aa1WqkvT", lambda: np.ascontiguousarray(
            np.asarray(inputs["L1_attn_a_w_qkv"], np.float32).T)),
        "WprojT": prog.dram_in_cached("aa1WprojT", lambda: np.ascontiguousarray(
            np.asarray(inputs["L1_attn_a_w_proj"], np.float32).T)),
        "diag_mask": prog.dram_in_cached("c_dmsk", lambda: c["diag_mask01"]),
        "ones_col_bf": prog.dram_in_cached("c_ocb", lambda: c["ones_col_bf"]),
        "ones_k1_bf": prog.dram_in_cached("c_okb", lambda: c["ones_k1_bf"]),
    })
    ph = Ph(prog, "a1", prev=prev)
    with prog.scope():
        return phase_aa1(prog, ph, xin, xout, w)


def emit_ab1(prog, inputs, xin, xout, prev=None):
    c = consts()
    w = dict(w_common(prog))
    w.update({
        "WqkvT": prog.dram_in_cached("ab1WqkvT", lambda: np.ascontiguousarray(
            np.asarray(inputs["L1_attn_b_qkv_w"], np.float32).T)),
        "WoutT": prog.dram_in_cached("ab1WoutT", lambda: np.ascontiguousarray(
            np.asarray(inputs["L1_attn_b_out_w"], np.float32).T)),
        "cos_tm": prog.dram_in_cached("c_cos", lambda: c["rope_cos_tm"]),
        "sin_tm": prog.dram_in_cached("c_sin", lambda: c["rope_sin_tm"]),
        "qkvb_rep": prog.dram_in_cached("ab1qb", lambda: rep(np.asarray(inputs["L1_attn_b_qkv_b"], np.float32))),
        "outb_rep": prog.dram_in_cached("ab1ob", lambda: rep(np.asarray(inputs["L1_attn_b_out_b"], np.float32))),
        "ones_col_bf": prog.dram_in_cached("c_ocb", lambda: c["ones_col_bf"]),
        "ones_k1_bf": prog.dram_in_cached("c_okb", lambda: c["ones_k1_bf"]),
    })
    ph = Ph(prog, "b1", prev=prev)
    with prog.scope():
        return phase_ab1(prog, ph, xin, xout, w)


_MOE_CAST_DONE = [False]

def emit_moe(prog, inputs, xin, xout, nm, prev=None):
    c = consts()
    w = dict(w_common(prog))
    first = "moepre" not in prog._names
    w.update({
        "W_preT": prog.dram_in_cached("moeWpreT", lambda: np.ascontiguousarray(
            np.asarray(inputs["L0_ffn_ff_pre_act_weight"], np.float32).transpose(0, 2, 1))),
        "W_gateT": prog.dram_in_cached("moeWgateT", lambda: np.ascontiguousarray(
            np.asarray(inputs["L0_ffn_gate_weight"], np.float32).transpose(0, 2, 1))),
        "W_postT": prog.dram_in_cached("moeWpostT", lambda: np.ascontiguousarray(
            np.asarray(inputs["L0_ffn_ff_post_act_weight"], np.float32).transpose(0, 2, 1))),
        "W_rT": prog.dram_in_cached("moeWrT", lambda: np.ascontiguousarray(
            np.asarray(inputs["L0_ffn_router_weight"], np.float32).T)),
        "slab_init": prog.dram_in_cached("c_slab", lambda: c["slab_init"]),
        "iota8": prog.dram_in_cached("c_io8", lambda: c["iota8"]),
        "ones_col": prog.dram_in_cached("c_oc", lambda: c["ones_col"]),
        "ones_k1": prog.dram_in_cached("c_ok", lambda: c["ones_k1"]),
        "Lstrict": prog.dram_in_cached("c_Ls", lambda: c["Lstrict"]),
        "tok_i32": prog.dram_in_cached("c_tok", lambda: c["tok_i32"]),
        "tok2a_i32": prog.dram_in_cached("c_t2a", lambda: c["tok2a_i32"]),
        "tok2b_i32": prog.dram_in_cached("c_t2b", lambda: c["tok2b_i32"]),
    })
    if first:
        prog._names.add("moepre")
        prog._cache["moepre_bf"] = prog.dram("moepre_bf", [8 * D, 4096], BF16)
        prog._cache["moegate_bf"] = prog.dram("moegate_bf", [8 * D, 4096], BF16)
        prog._cache["moepost_bf"] = prog.dram("moepost_bf", [8 * 4096, D], BF16)
        prog._cache["moewrT_bf"] = prog.dram("moewrT_bf", [D, 8], BF16)
    w["preT_bf"] = prog._cache["moepre_bf"]
    w["gateT_bf"] = prog._cache["moegate_bf"]
    w["postT_bf"] = prog._cache["moepost_bf"]
    w["wrT_bf"] = prog._cache["moewrT_bf"]
    w["do_cast"] = first
    ph = Ph(prog, nm, prev=prev)
    return phase_moe(prog, ph, xin, xout, w)


LAUNCH_PLAN = ["gla", "moe_a", "ab0", "moe_b", "aa1", "ffn_a", "ab1", "ffn_b"]


def run_launch(which, inputs, xs):
    prog = Prog()
    xin = prog.dram("x_in", [NTOK, D], F32, kind="ExternalInput")
    xout = prog.dram("x_out", [NTOK, D], F32, kind="ExternalOutput")
    if which == "gla":
        emit_gla(prog, inputs, xin, xout)
    elif which.startswith("moe"):
        emit_moe(prog, inputs, xin, xout, "m" + which[-1])
    elif which == "ab0":
        emit_ab0(prog, inputs, xin, xout)
    elif which == "aa1":
        emit_aa1(prog, inputs, xin, xout)
    elif which.startswith("ffn"):
        emit_ffn(prog, inputs, xin, xout, "f" + which[-1])
    elif which == "ab1":
        emit_ab1(prog, inputs, xin, xout)
    else:
        raise ValueError(which)
    prog.close()
    in_maps = []
    for cc in range(NCORES):
        m = dict(prog.feeds)
        m["x_in"] = xs[cc]
        in_maps.append(m)
    res = run_bass_kernel_spmd(prog.nc, in_maps, core_ids=list(range(NCORES)))
    return [np.asarray(r["x_out"]) for r in res.results]


def kernel(**inputs):
    xs = shard_inputs(inputs)
    for which in LAUNCH_PLAN:
        xs = run_launch(which, inputs, xs)
    return np.stack(xs).reshape(16, 2048, 1024)
